# version 02
# baseline (speedup 1.0000x reference)
import numpy as np
import ml_dtypes
import concourse.bass as bass
import concourse.mybir as mybir
from concourse.bass_utils import run_bass_kernel_spmd

F32 = mybir.dt.float32
BF16 = mybir.dt.bfloat16
AF = mybir.ActivationFunctionType
ALU = mybir.AluOpType
BF16NP = ml_dtypes.bfloat16

E = 2048
S = 4096
NB = 4
NHL = 8
HD = 128
INTER = 8192
NFC = 16
NIC = 64
TT = 512
NT_FULL = 8
NT_OWN = 4
RS2048 = float(1.0 / np.sqrt(2048.0))
RS8192 = float(1.0 / np.sqrt(8192.0))
SMS = float(1.0 / np.sqrt(128.0))
EPS = 1e-5


def _had(n):
    i = np.arange(n)[:, None]
    j = np.arange(n)[None, :]
    return ((-1.0) ** np.bitwise_count(i & j)).astype(np.float32)


def pack_lhsT(w):
    K, M = w.shape
    return np.ascontiguousarray(
        w.reshape(K // 128, 128, M // 128, 128).transpose(2, 1, 0, 3))


def pack_rhs(w):
    K, M = w.shape
    return np.ascontiguousarray(w.reshape(K // 128, 128, M))


def _quads(nbits_total, sp):
    n = 1 << nbits_total
    b0, b1 = 1 << (2 * sp), 1 << (2 * sp + 1)
    out = []
    for c in range(n):
        if (c & b0) == 0 and (c & b1) == 0:
            out.append((c, c + b0, c + b1, c + b0 + b1))
    return out


class Prog:

    def __init__(self, nc, layers, T, C, SB, PS, SEM):
        self.nc = nc
        self.layers = layers
        self.T = T
        self.C = C
        self.SB = SB
        self.PS = PS
        self.SEM = SEM

    def start(self, role, eng):
        self.role = role
        self.eng = eng
        self.stoplim = getattr(self, "stoplim", None)
        self.skipset = getattr(self, "skipset", None)
        self.c = {k: 0 for k in ("qld", "qst", "qgp", "cc", "spe", "sact",
                                 "sdve", "qp0", "qp1", "qs0", "qs1",
                                 "qq0", "qq1", "bpe", "bact", "bdve",
                                 "bpool", "bsync", "phase")}
        self.lastwait = {}
        self.lu = {}

    def on(self, *roles):
        return self.role in roles

    def _wait(self, sem, val):
        if val <= 0:
            return
        key = sem
        if self.lastwait.get(key, -1) >= val:
            return
        self.lastwait[key] = val
        self.eng.wait_ge(self.SEM[sem], val)

    def w(self, role, *sems):
        if self.on(role):
            for s in sems:
                self._wait(s, self.c[s])

    def wv(self, role, sem, val):
        if self.on(role):
            self._wait(sem, val)

    def dma(self, q, out, in_, sem=None):
        qsem = sem or {"sync": "qld", "scalar": "qst", "pool": "qgp"}[q]
        qrole = {"sync": "sync", "scalar": "act", "pool": "pool"}[q]
        if self.on(qrole):
            self.eng.dma_start(out=out, in_=in_).then_inc(self.SEM[qsem], 16)
        self.c[qsem] += 16

    def mm(self, out, lhsT, rhs, start, stop, inc=False, skip=False):
        if self.on("pe"):
            i = self.eng.matmul(out, lhsT, rhs, start=start, stop=stop,
                                skip_group_check=skip)
            if inc:
                i.then_inc(self.SEM["spe"], 1)
        if inc:
            self.c["spe"] += 1

    def op(self, role, fn, inc=None):
        auto = {"dve": "sdve", "act": "sact"}.get(role)
        if self.on(role):
            if auto:
                self._wait(auto, self.c[auto])
            i = fn(self.eng)
            if auto:
                i.then_inc(self.SEM[auto], 1)
            if inc and inc != auto:
                i.then_inc(self.SEM[inc], 1)
        if auto:
            self.c[auto] += 1
        if inc and inc != auto:
            self.c[inc] += 1

    def barrier(self):
        self.c["phase"] += 1
        ph = self.c["phase"]
        mine = {"pe": "bpe", "act": "bact", "dve": "bdve",
                "pool": "bpool", "sync": "bsync"}[self.role] \
            if self.role in ("pe", "act", "dve", "pool", "sync") else None
        for s in ("qld", "qst", "qgp", "qp0", "qp1", "qs0", "qs1",
                  "qq0", "qq1", "cc"):
            self._wait(s, self.c[s])
        if self.role in ("pe", "act", "dve", "pool"):
            self.eng.drain()
        self.eng.sem_inc(self.SEM[mine], 1)
        for b in ("bpe", "bact", "bdve", "bpool", "bsync"):
            self.c[b] = ph
            if b != mine:
                self._wait(b, ph)

    def chk(self):
        self.step += 1
        if self.skipset and self.step in self.skipset:
            return False
        return self.stoplim is None or self.step <= self.stoplim

    def program(self):
        self.step = 0
        self.stopped = False
        self.ph_setup()
        self.barrier()
        for L in self.layers:
            if L == 0:
                if self.chk():
                    self.ph_ln(0, "full")
                self.barrier()
                if self.chk():
                    self.ph_ln(0, "own")
            else:
                self.chk()
                if self.chk():
                    self.ph_ln(1, "own")
                    self.w("pool", "qst")
                    self.op("pool", lambda e: e.collective_compute(
                        "AllGather", ALU.bypass,
                        replica_groups=[[0, 1], [2, 3], [4, 5], [6, 7]],
                        ins=[self.T["xn1_own_cc"][:, :]],
                        outs=[self.T["xn1_full_cc"][:, :]]), inc="cc")
            self.barrier()
            if self.chk():
                self.ph_ffn1(L)
            self.barrier()
            if self.chk():
                self.ph_ffn2(L)
            self.barrier()
            if self.chk():
                self.ph_qkv(L)
            self.barrier()
            if self.chk():
                self.ph_bounds(L)
            self.barrier()
            if self.chk():
                self.ph_attn(L)
            self.barrier()
            if self.chk():
                self.ph_outproj(L)
            self.barrier()
            if self.chk():
                self.ph_resid(L)
            self.barrier()

    def ph_setup(self):
        T, C, SB = self.T, self.C, self.SB
        self.dma("sync", SB["h128"][:, :], C["h128"])
        self.dma("sync", SB["ones"][:, :], C["ones"])
        for o in range(4):
            self.dma("sync", SB["mask"][:, o, :], C["mask"][o])
        if 0 in self.layers:
            self.dma("sync", SB["su"][:, :], T["su_ug_t"])
            self.dma("sync", SB["mu"][:, 0:128], T["wsc_ug_t"])
            self.dma("sync", SB["mu"][:, 128:192], T["su_dp_t"])
            self.dma("sync", SB["mu"][:, 192:208], T["wsc_dp_t"])
            self.dma("sync", SB["bias"][:, 0:8], T["bq0_t"])
            self.dma("sync", SB["bias"][:, 8:9], T["bk0_t"])
            self.dma("sync", SB["bias"][:, 9:25], T["bfc_t"])
            self.dma("pool", SB["bvrow"][:, :], T["bv0_row"])
        self.w("dve", "qld", "qgp")
        A = SB["alpha"]
        W = SB["mu"]
        if 0 in self.layers:
            self.op("dve", lambda e: e.tensor_mul(A[:, 0:64], W[:, 0:64],
                                                  W[:, 128:192]))
            self.op("dve", lambda e: e.tensor_scalar_mul(A[:, 0:64], A[:, 0:64],
                                                         RS2048))
            self.op("dve", lambda e: e.tensor_scalar_mul(A[:, 64:128],
                                                         W[:, 64:128], RS2048))
            self.op("dve", lambda e: e.tensor_scalar_mul(A[:, 128:144],
                                                         W[:, 192:208], RS8192))
        self.op("dve", lambda e: e.memset(A[:, 144:145], 1.0))
        self.op("dve", lambda e: e.memset(self.SB["nt"][:, 6:7], EPS))
        self.op("dve", lambda e: e.memset(SB["bias"][:, 25:26], 0.0))

    def ph_ln(self, L, kind):
        T, SB, PS = self.T, self.SB, self.PS
        if L == 0 and kind == "full":
            ntiles, fwht, to_act = NT_FULL, False, False
            src = lambda t, fc: T["x0T_full"][fc * 128:(fc + 1) * 128,
                                              t * TT:(t + 1) * TT]
            dst = lambda t, fc: T["xnT_hbm"][fc * 128:(fc + 1) * 128,
                                             t * TT:(t + 1) * TT]
        elif L == 0:
            ntiles, fwht, to_act = NT_OWN, True, True
            src = lambda t, fc: T["x0T_own"][fc * 128:(fc + 1) * 128,
                                             t * TT:(t + 1) * TT]
            dst = None
        else:
            ntiles, fwht, to_act = NT_OWN, False, True
            src = lambda t, fc: T["x1T"][fc * 128:(fc + 1) * 128,
                                         t * TT:(t + 1) * TT]
            dst = lambda t, fc: T["xn1_own_cc"][fc * 128:(fc + 1) * 128,
                                                t * TT:(t + 1) * TT]
        AB = SB["actbuf"]
        for t in range(ntiles):
            self.w("sync", "sdve")
            for fc in range(NFC):
                self.dma("sync", SB["res32"][:, fc, :], src(t, fc))
            for fc in range(NFC):
                sl = fc % 2
                if fc == 0:
                    self.w("dve", "qld")
                self.w("dve", "spe")
                self.op("dve", lambda e, fc=fc, sl=sl: e.tensor_copy(
                    SB["xcb"][:, sl, :], SB["res32"][:, fc, :]))
                self.op("dve", lambda e, sl=sl: e.tensor_mul(
                    SB["sqb"][:, sl, :], SB["xcb"][:, sl, :],
                    SB["xcb"][:, sl, :]))
                self.w("pe", "sdve")
                if fc == 0:
                    self.w("pe", "sact")
                self.mm(PS[0][:, :], SB["ones"][:, :], SB["xcb"][:, sl, :],
                        start=(fc == 0), stop=(fc == NFC - 1))
                self.mm(PS[1][:, :], SB["ones"][:, :], SB["sqb"][:, sl, :],
                        start=(fc == 0), stop=(fc == NFC - 1),
                        inc=True)
            self.w("dve", "spe")
            self.op("dve", lambda e: e.tensor_scalar_mul(SB["mu"][:, :],
                                                         PS[0][:, :], 1.0 / E))
            self.op("dve", lambda e: e.tensor_scalar_mul(SB["var"][:, :],
                                                         PS[1][:, :], 1.0 / E))
            self.op("dve", lambda e: e.tensor_mul(SB["lnt"][:, :],
                                                  SB["mu"][:, :],
                                                  SB["mu"][:, :]))
            self.op("dve", lambda e: e.tensor_sub(SB["var"][:, :],
                                                  SB["var"][:, :],
                                                  SB["lnt"][:, :]))
            self.w("act", "sdve")
            self.op("act", lambda e: e.activation(SB["lnt"][:, :],
                                                  SB["var"][:, :], AF.Ln,
                                                  bias=SB["nt"][:, 6:7]))
            self.op("act", lambda e: e.activation(SB["rstd"][:, :],
                                                  SB["lnt"][:, :], AF.Exp,
                                                  scale=-0.5))
            self.w("dve", "sact", "qst")
            for fc in range(NFC):
                sl = fc % 2
                self.op("dve", lambda e, fc=fc, sl=sl: e.tensor_sub(
                    SB["xf"][:, 0, :], SB["res32"][:, fc, :], SB["mu"][:, :]))
                self.op("dve", lambda e, fc=fc, sl=sl: e.tensor_mul(
                    SB["xnb"][:, fc, :], SB["xf"][:, 0, :], SB["rstd"][:, :]))
                if to_act:
                    dstap = AB[:, fc * 2048 + t * TT: fc * 2048 + t * TT + TT]
                    if L == 0:
                        self.op("dve", lambda e, fc=fc, d=dstap: e.tensor_scalar_mul(
                            d, SB["xnb"][:, fc, :], SB["su"][:, fc:fc + 1]))
                    else:
                        self.op("dve", lambda e, fc=fc, d=dstap: e.tensor_copy(
                            d, SB["xnb"][:, fc, :]))
            self.op("dve", lambda e: e.engine_nop())
            if dst is not None:
                self.w("act", "sdve")
                for fc in range(NFC):
                    self.dma("scalar", dst(t, fc), SB["xnb"][:, fc, :])
            if fwht:
                A = lambda cc: AB[:, cc * 2048 + t * TT: cc * 2048 + t * TT + TT]
                BT = SB["btmp"]
                for sp in range(2):
                    for (c0, c1, c2, c3) in _quads(4, sp):
                        self.op("dve", lambda e, a=A(c0), b=A(c1): e.tensor_add(BT[:, 0, :], a, b))
                        self.op("dve", lambda e, a=A(c0), b=A(c1): e.tensor_sub(BT[:, 1, :], a, b))
                        self.op("dve", lambda e, a=A(c2), b=A(c3): e.tensor_add(BT[:, 2, :], a, b))
                        self.op("dve", lambda e, a=A(c2), b=A(c3): e.tensor_sub(BT[:, 3, :], a, b))
                        self.op("dve", lambda e, o=A(c0): e.tensor_add(o, BT[:, 0, :], BT[:, 2, :]))
                        self.op("dve", lambda e, o=A(c2): e.tensor_sub(o, BT[:, 0, :], BT[:, 2, :]))
                        self.op("dve", lambda e, o=A(c1): e.tensor_add(o, BT[:, 1, :], BT[:, 3, :]))
                        self.op("dve", lambda e, o=A(c3): e.tensor_sub(o, BT[:, 1, :], BT[:, 3, :]))
                self.op("dve", lambda e: e.engine_nop())
                for fc in range(NFC):
                    pb = 2 + (fc % 2)
                    self.w("pe", "sdve")
                    self.mm(self.PS[pb][:, :], SB["h128"][:, :], A(fc),
                            start=True, stop=True, inc=True)
                    self.w("dve", "spe")
                    self.op("dve", lambda e, fc=fc, pb=pb: e.tensor_copy(
                        A(fc), self.PS[pb][:, :]))

    def ph_ffn1(self, L):
        T, SB, PS = self.T, self.SB, self.PS
        AB = SB["actbuf"]
        wup = T["wup_pk"][L]
        wgt = T["wgate_pk"][L]
        for ob in range(NIC):
            sl = ob % 2
            pu = self.lu.get(("wblk", sl))
            if pu:
                self.wv("pool", *pu)
            qp = f"qp{sl}"
            self.dma("pool", SB["wblk"][:, sl, :, :], wup[ob], sem=qp)
            self.dma("pool", SB["wblk"][:, 2 + sl, :, :], wgt[ob], sem=qp)
            ldc = self.c[qp]
            for ts in range(4):
                bg = (ob * 4 + ts) % 2
                bu = 2 + (ob * 4 + ts) % 2
                if ts == 0:
                    self.wv("pe", qp, ldc)
                self.w("pe", "sact")
                for kc in range(NFC):
                    self.mm(PS[bg][:, :], SB["wblk"][:, 2 + sl, kc, :],
                            AB[:, kc * 2048 + ts * TT: kc * 2048 + ts * TT + TT],
                            start=(kc == 0), stop=(kc == NFC - 1),
                            inc=(kc == NFC - 1))
                self.w("act", "spe")
                gsc = SB["alpha"][:, 64 + ob:64 + ob + 1] if L == 0 else 1.0
                self.op("act", lambda e, bg=bg, gsc=gsc: e.activation(
                    SB["silu"][:, (0 if bg == 0 else 1), :], PS[bg][:, :],
                    AF.Silu, scale=gsc))
                self.w("pe", "sdve")
                for kc in range(NFC):
                    self.mm(PS[bu][:, :], SB["wblk"][:, sl, kc, :],
                            AB[:, kc * 2048 + ts * TT: kc * 2048 + ts * TT + TT],
                            start=(kc == 0), stop=(kc == NFC - 1),
                            inc=(kc == NFC - 1))
                ysl = (ob * 4 + ts) % 2
                self.w("dve", "spe", "sact", "qst")
                usc = SB["alpha"][:, ob:ob + 1] if L == 0 \
                    else SB["alpha"][:, 144:145]
                self.op("dve", lambda e, bu=bu, bg=bg, ysl=ysl, usc=usc:
                        e.scalar_tensor_tensor(
                            SB["ybuf"][:, ysl, :], PS[bu][:, :], usc,
                            SB["silu"][:, (0 if bg == 0 else 1), :],
                            ALU.mult, ALU.mult))
                self.w("act", "sdve")
                self.dma("scalar", T["yH"][ob][:, ts * TT:(ts + 1) * TT],
                         SB["ybuf"][:, ysl, :])
            self.lu[("wblk", sl)] = ("spe", self.c["spe"])

    def ph_ffn2(self, L):
        T, SB, PS = self.T, self.SB, self.PS
        AB = SB["actbuf"]
        wdn = T["wdn_pk"][L]
        res_src = (lambda t, fc: T["x0T_own"][fc * 128:(fc + 1) * 128,
                                              t * TT:(t + 1) * TT]) if L == 0 \
            else (lambda t, fc: T["x1T"][fc * 128:(fc + 1) * 128,
                                         t * TT:(t + 1) * TT])
        for t in range(NT_OWN):
            self.w("sync", "spe", "sdve")
            for ic in range(NIC):
                self.dma("sync", AB[:, ic * TT:(ic + 1) * TT],
                         T["yH"][ic][:, t * TT:(t + 1) * TT])
            for fc in range(NFC):
                self.dma("sync", SB["res32"][:, fc, :], res_src(t, fc))
            self.w("dve", "qld")
            A = lambda cc: AB[:, cc * TT:(cc + 1) * TT]
            if L == 0:
                BT = SB["btmp"]
                for sp in range(3):
                    for (c0, c1, c2, c3) in _quads(6, sp):
                        self.op("dve", lambda e, a=A(c0), b=A(c1): e.tensor_add(BT[:, 0, :], a, b))
                        self.op("dve", lambda e, a=A(c0), b=A(c1): e.tensor_sub(BT[:, 1, :], a, b))
                        self.op("dve", lambda e, a=A(c2), b=A(c3): e.tensor_add(BT[:, 2, :], a, b))
                        self.op("dve", lambda e, a=A(c2), b=A(c3): e.tensor_sub(BT[:, 3, :], a, b))
                        self.op("dve", lambda e, o=A(c0): e.tensor_add(o, BT[:, 0, :], BT[:, 2, :]))
                        self.op("dve", lambda e, o=A(c2): e.tensor_sub(o, BT[:, 0, :], BT[:, 2, :]))
                        self.op("dve", lambda e, o=A(c1): e.tensor_add(o, BT[:, 1, :], BT[:, 3, :]))
                        self.op("dve", lambda e, o=A(c3): e.tensor_sub(o, BT[:, 1, :], BT[:, 3, :]))
                self.op("dve", lambda e: e.engine_nop())
                for ic in range(NIC):
                    pb = 2 + (ic % 2)
                    self.w("pe", "sdve")
                    self.mm(PS[pb][:, :], SB["h128"][:, :], A(ic),
                            start=True, stop=True, inc=True)
                    self.w("dve", "spe")
                    self.op("dve", lambda e, ic=ic, pb=pb: e.tensor_copy(
                        A(ic), PS[pb][:, :]))
            self.w("pe", "qld")
            for mb in range(NFC):
                sl = mb % 2
                pu = self.lu.get(("wdnb", sl))
                if pu:
                    self.wv("pool", *pu)
                off = sl * 8192
                qp = f"qp{sl}"
                self.dma("pool", SB["buf32k"][:, off:off + 8192],
                         wdn[mb].rearrange("p a b -> p (a b)"), sem=qp)
                ldc = self.c[qp]
                pb = 4 + (mb % 2)
                self.wv("pe", qp, ldc)
                self.w("pe", "sdve")
                for ic in range(NIC):
                    self.mm(PS[pb][:, :],
                            SB["buf32k"][:, off + ic * 128: off + (ic + 1) * 128],
                            A(ic), start=(ic == 0), stop=(ic == NIC - 1),
                            inc=(ic == NIC - 1))
                self.lu[("wdnb", sl)] = ("spe", self.c["spe"])
                dsc = SB["alpha"][:, 128 + mb:128 + mb + 1] if L == 0 \
                    else SB["alpha"][:, 144:145]
                osl = mb % 2
                self.w("dve", "spe", "qst")
                self.op("dve", lambda e, pb=pb, mb=mb, osl=osl, dsc=dsc:
                        e.scalar_tensor_tensor(
                            SB["o32"][:, osl, :], PS[pb][:, :], dsc,
                            SB["res32"][:, mb, :], ALU.mult, ALU.add))
                self.w("act", "sdve")
                self.dma("scalar",
                         T["x1pT"][mb * 128:(mb + 1) * 128, t * TT:(t + 1) * TT],
                         SB["o32"][:, osl, :])

    def ph_qkv(self, L):
        T, SB, PS = self.T, self.SB, self.PS
        AB = SB["actbuf"]
        DV = 128 if L == 0 else 1024
        nkh = 1 if L == 0 else NHL
        for kc in range(NFC):
            self.dma("pool", SB["buf32k"][:, kc * DV:(kc + 1) * DV],
                     T["wv_rhs"][L][kc])
        for th in range(2):
            self.w("sync", "spe")
            if L == 1 and th == 0:
                self.wv("sync", "cc", self.c["cc"])
            for fc in range(NFC):
                if L == 0:
                    srcap = T["xnT_hbm"][fc * 128:(fc + 1) * 128,
                                         th * 2048:(th + 1) * 2048]
                else:
                    srcap = T["xn1_full_cc"][th * 2048 + fc * 128:
                                             th * 2048 + (fc + 1) * 128, :]
                self.dma("sync", AB[:, fc * 2048:(fc + 1) * 2048], srcap)
            ldx = self.c["qld"]
            for (nmb, wpk, dstT, bias) in (
                (NHL, T["wq_pk"][L], T["qT_hbm"],
                 SB["bias"][:, 0:8] if L == 0 else None),
                (nkh, T["wk_pk"][L], T["kT_hbm"],
                 SB["bias"][:, 8:9] if L == 0 else None),
            ):
                for mb in range(nmb):
                    sl = mb % 2
                    pu = self.lu.get(("wblk", sl))
                    if pu:
                        self.wv("pool", *pu)
                    qp = f"qp{sl}"
                    self.dma("pool", SB["wblk"][:, sl, :, :], wpk[mb], sem=qp)
                    ldc = self.c[qp]
                    for tq in range(4):
                        pb = tq % 2
                        if tq == 0:
                            self.wv("pe", qp, ldc)
                            self.wv("pe", "qld", ldx)
                        self.w("pe", "sdve")
                        for kc in range(NFC):
                            self.mm(PS[pb][:, :], SB["wblk"][:, sl, kc, :],
                                    AB[:, kc * 2048 + tq * TT:
                                       kc * 2048 + tq * TT + TT],
                                    start=(kc == 0), stop=(kc == NFC - 1),
                                    inc=(kc == NFC - 1))
                        esl = tq % 2
                        self.w("dve", "spe", "qst")
                        if bias is not None:
                            bcol = bias[:, mb:mb + 1]
                            self.op("dve", lambda e, pb=pb, esl=esl, b=bcol:
                                    e.tensor_scalar_add(SB["qev"][:, esl, :],
                                                        PS[pb][:, :], b))
                        else:
                            self.op("dve", lambda e, pb=pb, esl=esl:
                                    e.tensor_copy(SB["qev"][:, esl, :],
                                                  PS[pb][:, :]))
                        self.w("act", "sdve")
                        self.dma("scalar",
                                 dstT[mb * 128:(mb + 1) * 128,
                                      th * 2048 + tq * TT:
                                      th * 2048 + (tq + 1) * TT],
                                 SB["qev"][:, esl, :])
                    self.lu[("wblk", sl)] = ("spe", self.c["spe"])
            for tsub in range(16):
                for dvg in range(DV // 512 if DV >= 512 else 1):
                    pb = 4 + (tsub % 2)
                    n = min(DV, 512)
                    if tsub == 0 and dvg == 0:
                        self.w("pe", "qgp")
                    self.w("pe", "sdve")
                    if L == 0:
                        self.mm(PS[pb][:, 0:128], SB["ones"][0:1, :],
                                SB["bvrow"][:, :], start=True, stop=False,
                                skip=True)
                    for kc in range(NFC):
                        self.mm(PS[pb][:, 0:n],
                                AB[:, kc * 2048 + tsub * 128:
                                   kc * 2048 + tsub * 128 + 128],
                                SB["buf32k"][:, kc * DV + dvg * 512:
                                             kc * DV + dvg * 512 + n],
                                start=(kc == 0 and L == 1), stop=(kc == NFC - 1),
                                inc=(kc == NFC - 1), skip=(L == 0))
                    esl = tsub % 2
                    self.w("dve", "spe", "qst")
                    self.op("dve", lambda e, pb=pb, esl=esl, n=n: e.tensor_copy(
                        SB["qev"][:, esl, 0:n], PS[pb][:, 0:n]))
                    self.w("act", "sdve")
                    for hj in range(n // 128):
                        head = dvg * 4 + hj
                        tc = th * 16 + tsub
                        self.dma("scalar", T["v_hbm"][head, tc],
                                 SB["qev"][:, esl, hj * 128:(hj + 1) * 128])

    def ph_bounds(self, L):
        T, SB, PS = self.T, self.SB, self.PS
        nkh = 1 if L == 0 else NHL
        for (nh, srcT, dst) in ((NHL, T["qT_hbm"], SB["mq2"]),
                                (nkh, T["kT_hbm"], SB["mk2"])):
            for h in range(nh):
                self.w("sync", "sdve")
                self.dma("sync", SB["buf32k"][:, 0:4096],
                         srcT[h * 128:(h + 1) * 128, :])
                self.w("dve", "qld")
                for g in range(8):
                    sl = g % 2
                    self.w("dve", "spe")
                    self.op("dve", lambda e, g=g, sl=sl: e.tensor_mul(
                        SB["sqb"][:, sl, :],
                        SB["buf32k"][:, g * TT:(g + 1) * TT],
                        SB["buf32k"][:, g * TT:(g + 1) * TT]))
                    pb = g % 2
                    self.w("pe", "sdve")
                    self.w("pe", "sact")
                    self.mm(PS[pb][:, :], SB["ones"][:, :], SB["sqb"][:, sl, :],
                            start=True, stop=True, inc=True)
                    self.w("dve", "spe")
                    self.op("dve", lambda e, pb=pb: e.reduce_max(
                        SB["nt"][:, 0:1], PS[pb][:, :], mybir.AxisListType.X))
                    if g == 0:
                        self.op("dve", lambda e, h=h, dst=dst: e.tensor_copy(
                            dst[:, h:h + 1], SB["nt"][:, 0:1]))
                    else:
                        self.op("dve", lambda e, h=h, dst=dst: e.tensor_max(
                            dst[:, h:h + 1], dst[:, h:h + 1], SB["nt"][:, 0:1]))
        self.w("act", "sdve")
        for h in range(NHL):
            kh = 0 if L == 0 else h
            self.op("act", lambda e, h=h: e.activation(
                SB["nt"][:, 1:2], SB["mq2"][:, h:h + 1], AF.Ln))
            self.op("act", lambda e, kh=kh: e.activation(
                SB["nt"][:, 2:3], SB["mk2"][:, kh:kh + 1], AF.Ln))
            self.w("dve", "sact")
            self.op("dve", lambda e: e.tensor_add(
                SB["nt"][:, 3:4], SB["nt"][:, 1:2], SB["nt"][:, 2:3]))
            self.w("act", "sdve")
            self.op("act", lambda e: e.activation(
                SB["nt"][:, 4:5], SB["nt"][:, 3:4], AF.Exp, scale=0.5))
            self.w("dve", "sact")
            self.op("dve", lambda e, h=h: e.tensor_scalar_mul(
                SB["bh"][:, h:h + 1], SB["nt"][:, 4:5], -SMS))

    def ph_attn(self, L):
        T, SB, PS = self.T, self.SB, self.PS
        for h in range(NHL):
            kv = 0 if L == 0 else h
            ksl = h % 2
            koff = ksl * 4096
            voff = 8192 + ksl * 4096
            qs = f"qs{ksl}"
            if not (L == 0 and h > 0):
                pu = self.lu.get(("kvs", ksl))
                if pu:
                    self.wv("sync", *pu)
                self.dma("sync", SB["buf32k"][:, koff:koff + 4096],
                         T["kT_hbm"][kv * 128:(kv + 1) * 128, :], sem=qs)
                for tc in range(32):
                    self.dma("sync", SB["buf32k"][:, voff + tc * 128:
                                                  voff + (tc + 1) * 128],
                             T["v_hbm"][kv, tc], sem=qs)
            if L == 0 and h > 0:
                koff = 0
                voff = 8192
                ksl = 0
                qs = "qs0"
            ldk = self.c[qs]
            for g in range(8):
                qsl = g % 2
                qq = f"qq{qsl}"
                self.w("sync", "spe")
                self.dma("sync", SB["qg"][:, qsl, :],
                         T["qT_hbm"][h * 128:(h + 1) * 128,
                                     g * TT:(g + 1) * TT], sem=qq)
                ldq = self.c[qq]
                nkc = 4 * g + 4
                bs_ = 2 + (g % 2)
                ba_ = 4 + (g % 2)
                for kc in range(nkc):
                    sb = kc % 2
                    if kc == 0:
                        self.wv("pe", qq, ldq)
                        self.wv("pe", qs, ldk)
                        self.w("pe", "sact")
                        self.w("pe", "sdve")
                    self.wv("pe", "sact", self.c["sact"])
                    self.mm(PS[sb][:, :],
                            SB["buf32k"][:, koff + kc * 128:koff + (kc + 1) * 128],
                            SB["qg"][:, qsl, :], start=True, stop=True,
                            inc=True)
                    psl = kc % 3
                    self.w("act", "spe")
                    self.op("act", lambda e, sb=sb, psl=psl, h=h: e.activation(
                        SB["pbuf"][:, psl, :], PS[sb][:, :], AF.Exp,
                        bias=SB["bh"][:, h:h + 1], scale=SMS))
                    if kc >= 4 * g:
                        o = kc - 4 * g
                        msl = kc % 2
                        self.w("dve", "sact")
                        self.op("dve", lambda e, psl=psl, msl=msl, o=o:
                                e.tensor_mul(SB["pmask"][:, msl, :],
                                             SB["pbuf"][:, psl, :],
                                             SB["mask"][:, o, :]))
                        used = SB["pmask"][:, msl, :]
                        self.w("pe", "sdve")
                    else:
                        used = SB["pbuf"][:, psl, :]
                        self.w("pe", "sact")
                    self.mm(PS[bs_][:, :], SB["ones"][:, :], used,
                            start=(kc == 0), stop=(kc == nkc - 1))
                    self.mm(PS[ba_][:, :],
                            SB["buf32k"][:, voff + kc * 128:voff + (kc + 1) * 128],
                            used, start=(kc == 0), stop=(kc == nkc - 1),
                            inc=True)
                self.w("act", "spe", "sdve")
                self.op("act", lambda e, bs_=bs_: e.activation(
                    SB["lnt"][:, :], PS[bs_][:, :], AF.Ln))
                self.op("act", lambda e: e.activation(
                    SB["rsum"][:, 0, :], SB["lnt"][:, :], AF.Exp,
                    scale=-1.0))
                self.w("dve", "sact", "qst")
                self.op("dve", lambda e, ba_=ba_, g=g: e.tensor_mul(
                    SB["attn_ev"][:, g % 2, :], PS[ba_][:, :],
                    SB["rsum"][:, 0, :]))
                self.w("act", "sdve")
                self.dma("scalar",
                         T["attnT_hbm"][h * 128:(h + 1) * 128,
                                        g * TT:(g + 1) * TT],
                         SB["attn_ev"][:, g % 2, :])
            self.lu[("kvs", h % 2)] = ("spe", self.c["spe"])

    def ph_outproj(self, L):
        T, SB, PS = self.T, self.SB, self.PS
        AB = SB["actbuf"]
        wo = T["wo_pk"][L]
        for hc in range(NHL):
            self.dma("sync", AB[:, hc * 4096:(hc + 1) * 4096],
                     T["attnT_hbm"][hc * 128:(hc + 1) * 128, :])
        for mb in range(NFC):
            sl = mb % 2
            pu = self.lu.get(("wblk", sl))
            if pu:
                self.wv("pool", *pu)
            qp = f"qp{sl}"
            self.dma("pool", SB["wblk"][:, sl, 0:8, :], wo[mb], sem=qp)
            ldc = self.c[qp]
            for t8 in range(8):
                pb = t8 % 2
                if t8 == 0:
                    self.wv("pe", qp, ldc)
                    self.w("pe", "qld")
                self.w("pe", "sdve")
                for kc in range(NHL):
                    self.mm(PS[pb][:, :], SB["wblk"][:, sl, kc, :],
                            AB[:, kc * 4096 + t8 * TT:
                               kc * 4096 + t8 * TT + TT],
                            start=(kc == 0), stop=(kc == NHL - 1),
                            inc=(kc == NHL - 1))
                esl = t8 % 2
                self.w("dve", "spe", "qst")
                self.op("dve", lambda e, pb=pb, esl=esl: e.tensor_copy(
                    SB["qev"][:, esl, :], PS[pb][:, :]))
                self.w("act", "sdve")
                self.dma("scalar",
                         T["rs_in"][t8 // 4, mb * 128:(mb + 1) * 128,
                                    (t8 % 4) * TT:(t8 % 4 + 1) * TT],
                         SB["qev"][:, esl, :])
            self.lu[("wblk", sl)] = ("spe", self.c["spe"])
        self.w("pool", "qst")
        self.op("pool", lambda e: e.collective_compute(
            "ReduceScatter", ALU.add,
            replica_groups=[[0, 1], [2, 3], [4, 5], [6, 7]],
            ins=[T["rs_in"][:, :, :]], outs=[T["rs_out"][:, :]]), inc="cc")

    def ph_resid(self, L):
        T, SB = self.T, self.SB
        dstT = T["x1T"] if L == 0 else T["outT"]
        self.wv("sync", "cc", self.c["cc"])
        for t in range(NT_OWN):
            self.w("sync", "sdve")
            for fc in range(NFC):
                self.dma("sync", SB["xnb"][:, fc, :],
                         T["rs_out"][fc * 128:(fc + 1) * 128,
                                     t * TT:(t + 1) * TT])
                self.dma("sync", SB["res32"][:, fc, :],
                         T["x1pT"][fc * 128:(fc + 1) * 128,
                                   t * TT:(t + 1) * TT])
            self.w("dve", "qld")
            for fc in range(NFC):
                bcol = SB["bias"][:, 9 + fc:10 + fc] if L == 0 \
                    else SB["bias"][:, 25:26]
                osl = fc % 2
                self.w("dve", "qst")
                self.op("dve", lambda e, fc=fc, osl=osl, b=bcol:
                        e.scalar_tensor_tensor(
                            SB["o32"][:, osl, :], SB["xnb"][:, fc, :], b,
                            SB["res32"][:, fc, :], ALU.add, ALU.add))
                self.w("act", "sdve")
                self.dma("scalar",
                         dstT[fc * 128:(fc + 1) * 128, t * TT:(t + 1) * TT],
                         SB["o32"][:, osl, :])



def build_program(mode, taps=(), stoplim=None, skipset=None):
    layers = {"FUSED": [0, 1], "L0": [0], "L1": [1]}[mode]
    nc = bass.Bass("TRN2", target_bir_lowering=False, debug=False,
                   dynamic_dma_scratch_size=16384)
    T = {}

    def dram(name, shape, dtype, kind="Internal"):
        if name in taps and kind == "Internal":
            kind = "ExternalOutput"
        T[name] = nc.dram_tensor(name, shape, dtype, kind=kind).ap()
        return T[name]

    if 0 in layers:
        dram("x0T_full", [E, S], F32, "ExternalInput")
        dram("x0T_own", [E, 2048], F32, "ExternalInput")
        dram("su_ug_t", [128, 16], F32, "ExternalInput")
        dram("wsc_ug_t", [128, 128], F32, "ExternalInput")
        dram("su_dp_t", [128, 64], F32, "ExternalInput")
        dram("wsc_dp_t", [128, 16], F32, "ExternalInput")
        dram("bq0_t", [128, 8], F32, "ExternalInput")
        dram("bk0_t", [128, 1], F32, "ExternalInput")
        dram("bfc_t", [128, 16], F32, "ExternalInput")
        dram("bv0_row", [1, 128], F32, "ExternalInput")
    T["wq_pk"], T["wk_pk"], T["wv_rhs"], T["wo_pk"] = {}, {}, {}, {}
    T["wup_pk"], T["wgate_pk"], T["wdn_pk"] = {}, {}, {}
    for L in layers:
        T["wq_pk"][L] = dram(f"wq{L}", [8, 128, 16, 128], F32, "ExternalInput")
        T["wk_pk"][L] = dram(f"wk{L}", [1 if L == 0 else 8, 128, 16, 128],
                             F32, "ExternalInput")
        T["wv_rhs"][L] = dram(f"wv{L}", [16, 128, 128 if L == 0 else 1024],
                              F32, "ExternalInput")
        T["wo_pk"][L] = dram(f"wo{L}", [16, 128, 8, 128], F32, "ExternalInput")
        T["wup_pk"][L] = dram(f"wup{L}", [64, 128, 16, 128], F32,
                              "ExternalInput")
        T["wgate_pk"][L] = dram(f"wgate{L}", [64, 128, 16, 128], F32,
                                "ExternalInput")
        T["wdn_pk"][L] = dram(f"wdn{L}", [16, 128, 64, 128], F32,
                              "ExternalInput")
    if mode == "L0":
        dram("x1T", [E, 2048], F32, "ExternalOutput")
    elif mode == "L1":
        dram("x1T", [E, 2048], F32, "ExternalInput")
    else:
        dram("x1T", [E, 2048], F32, "Internal")
    if 1 in layers:
        dram("outT", [E, 2048], F32, "ExternalOutput")
    dram("xnT_hbm", [E, S], BF16)
    dram("qT_hbm", [1024, S], BF16)
    dram("kT_hbm", [1024, S], BF16)
    dram("v_hbm", [8, 32, 128, 128], BF16)
    dram("attnT_hbm", [1024, S], BF16)
    dram("x1pT", [E, 2048], F32)
    dram("yH", [64, 128, 2048], BF16)
    dram("rs_in", [2, E, 2048], BF16)
    dram("rs_out", [E, 2048], BF16)
    if "dbga" in taps:
        dram("dbga", [E, 2048], BF16)
    if 1 in layers:
        dram("xn1_own_cc", [E, 2048], BF16)
        dram("xn1_full_cc", [2 * E, 2048], BF16)

    C = {}
    C["h128"] = nc.inline_tensor(_had(128).astype(BF16NP), name="h128c").ap()
    C["ones"] = nc.inline_tensor(np.ones((128, 128), np.float32).astype(BF16NP),
                                 name="onesc").ap()
    mask = np.zeros((4, 128, 512), np.float32)
    for o in range(4):
        for p in range(128):
            mask[o, p, o * 128 + p:] = 1.0
    C["mask"] = nc.inline_tensor(mask.astype(BF16NP), name="maskc").ap()

    from contextlib import ExitStack
    _sb_specs = [
        ("actbuf", [128, 32768], BF16), ("buf32k", [128, 16384], BF16),
        ("wblk", [128, 4, 16, 128], BF16), ("res32", [128, 16, 512], F32),
        ("xnb", [128, 16, 512], BF16), ("xcb", [128, 2, 512], BF16),
        ("sqb", [128, 2, 512], BF16), ("xf", [128, 1, 512], F32),
        ("btmp", [128, 4, 512], BF16), ("mu", [128, 512], F32),
        ("var", [128, 512], F32), ("lnt", [128, 512], F32),
        ("rstd", [128, 512], F32), ("silu", [128, 2, 512], BF16),
        ("ybuf", [128, 2, 512], BF16), ("o32", [128, 2, 512], F32),
        ("qev", [128, 2, 512], BF16), ("qg", [128, 2, 512], BF16),
        ("pbuf", [128, 3, 512], BF16), ("pmask", [128, 2, 512], BF16),
        ("rsum", [128, 1, 512], F32), ("attn_ev", [128, 2, 512], BF16),
        ("mask", [128, 4, 512], BF16), ("h128", [128, 128], BF16),
        ("ones", [128, 128], BF16), ("su", [128, 16], F32),
        ("alpha", [128, 160], F32),
        ("bias", [128, 26], F32), ("bvrow", [1, 128], BF16),
        ("mq2", [128, 8], F32), ("mk2", [128, 8], F32),
        ("bh", [128, 8], F32), ("nt", [128, 8], F32),
    ]
    _sem_names = ("qld", "qst", "qgp", "cc", "spe", "sact", "sdve",
                  "qp0", "qp1", "qs0", "qs1", "qq0", "qq1",
                  "bpe", "bact", "bdve", "bpool", "bsync")
    with ExitStack() as _es:
        SB = {n: _es.enter_context(nc.sbuf_tensor(n, sh, dt))
              for (n, sh, dt) in _sb_specs}
        PS = [_es.enter_context(nc.psum_tensor(f"p{i}", [128, 512], F32))
              for i in range(6)]
        SEM = {n: _es.enter_context(nc.semaphore(n)) for n in _sem_names}
        block = _es.enter_context(nc.Block())
        prog = Prog(nc, layers, T, C, SB, PS, SEM)
        prog.stoplim = stoplim
        prog.skipset = skipset

        @block.tensor
        def _(eng):
            prog.start("pe", eng)
            prog.program()

        @block.scalar
        def _(eng):
            prog.start("act", eng)
            prog.program()

        @block.vector
        def _(eng):
            prog.start("dve", eng)
            prog.program()

        @block.gpsimd
        def _(eng):
            prog.start("pool", eng)
            prog.program()

        @block.sync
        def _(eng):
            prog.start("sync", eng)
            prog.program()

    return nc



def _stage_inputs(ins):
    per_core = [dict() for _ in range(8)]
    shared = {}

    w_qkv = np.asarray(ins["L0_attn_w_qkv"])
    b_qkv = np.asarray(ins["L0_attn_b_qkv"])
    w_fc = np.asarray(ins["L0_attn_w_fc"])
    b_fc = np.asarray(ins["L0_attn_b_fc"])
    shared["wk0"] = pack_lhsT(w_qkv[:, 2048:2176])
    shared["wv0"] = pack_rhs(w_qkv[:, 2176:2304])
    shared["bk0_t"] = np.ascontiguousarray(b_qkv[2048:2176].reshape(128, 1))
    shared["bv0_row"] = np.ascontiguousarray(b_qkv[2176:2304].reshape(1, 128))
    shared["bfc_t"] = np.ascontiguousarray(b_fc.reshape(16, 128).T)
    shared["su_ug_t"] = np.ascontiguousarray(
        np.asarray(ins["L0_ffn_SU_ug"]).reshape(16, 128).T)
    wsc = np.asarray(ins["L0_ffn_Wscale_ug"])
    shared["wsc_ug_t"] = np.ascontiguousarray(np.concatenate(
        [wsc[:8192].reshape(64, 128).T, wsc[8192:].reshape(64, 128).T], axis=1))
    shared["su_dp_t"] = np.ascontiguousarray(
        np.asarray(ins["L0_ffn_SU_dp"]).reshape(64, 128).T)
    shared["wsc_dp_t"] = np.ascontiguousarray(
        np.asarray(ins["L0_ffn_Wscale_dp"]).reshape(16, 128).T)
    shared["wup0"] = pack_lhsT(np.asarray(ins["L0_ffn_W_up"]).T)
    shared["wgate0"] = pack_lhsT(np.asarray(ins["L0_ffn_W_gate"]).T)
    shared["wdn0"] = pack_lhsT(np.asarray(ins["L0_ffn_W_down"]).T)
    Wq, Wk, Wv, Wo = (np.asarray(ins["L1_attn_Wq"]), np.asarray(ins["L1_attn_Wk"]),
                      np.asarray(ins["L1_attn_Wv"]), np.asarray(ins["L1_attn_Wo"]))
    wgu = np.asarray(ins["L1_ffn_w_gate_up"])
    shared["wgate1"] = pack_lhsT(wgu[:8192].T)
    shared["wup1"] = pack_lhsT(wgu[8192:].T)
    shared["wdn1"] = pack_lhsT(np.asarray(ins["L1_ffn_w_down"]).T)

    x0 = np.asarray(ins["L0_attn_x"])
    for c in range(8):
        b, par = c >> 1, c & 1
        m = per_core[c]
        x0T = np.ascontiguousarray(x0[b].T)
        m["x0T_full"] = x0T
        m["x0T_own"] = np.ascontiguousarray(x0T[:, par * 2048:(par + 1) * 2048])
        hc = slice(par * 1024, (par + 1) * 1024)
        m["wq0"] = pack_lhsT(w_qkv[:, hc])
        m["wk0"] = shared["wk0"]
        m["wv0"] = shared["wv0"]
        m["wo0"] = pack_lhsT(w_fc[hc, :])
        m["bq0_t"] = np.ascontiguousarray(b_qkv[hc].reshape(8, 128).T)
        for k in ("bk0_t", "bv0_row", "bfc_t", "su_ug_t", "wsc_ug_t",
                  "su_dp_t", "wsc_dp_t"):
            m[k] = shared[k]
        m["wup0"] = shared["wup0"]
        m["wgate0"] = shared["wgate0"]
        m["wdn0"] = shared["wdn0"]
        m["wq1"] = pack_lhsT(Wq[hc, :].T)
        m["wk1"] = pack_lhsT(Wk[hc, :].T)
        m["wv1"] = pack_rhs(Wv[hc, :].T)
        m["wo1"] = pack_lhsT(Wo.T[hc, :])
        m["wup1"] = shared["wup1"]
        m["wgate1"] = shared["wgate1"]
        m["wdn1"] = shared["wdn1"]
    return per_core


_L0_KEYS = ("x0T_full", "x0T_own", "su_ug_t", "wsc_ug_t", "su_dp_t",
            "wsc_dp_t", "bq0_t", "bk0_t", "bfc_t", "bv0_row",
            "wq0", "wk0", "wv0", "wo0", "wup0", "wgate0", "wdn0")
_L1_KEYS = ("wq1", "wk1", "wv1", "wo1", "wup1", "wgate1", "wdn1")

_MODE = "FUSED"


def kernel(**inputs):
    per_core = _stage_inputs(inputs)
    out = np.empty((NB, S, E), dtype=np.float32)
    if _MODE == "FUSED":
        nc = build_program("FUSED")
        in_maps = [{k: per_core[c][k] for k in _L0_KEYS + _L1_KEYS}
                   for c in range(8)]
        res = run_bass_kernel_spmd(nc, in_maps, core_ids=list(range(8)))
        for c in range(8):
            b, par = c >> 1, c & 1
            out[b, par * 2048:(par + 1) * 2048, :] = res.results[c]["outT"].T
    else:
        nc0 = build_program("L0")
        in_maps = [{k: per_core[c][k] for k in _L0_KEYS} for c in range(8)]
        res0 = run_bass_kernel_spmd(nc0, in_maps, core_ids=list(range(8)))
        nc1 = build_program("L1")
        in_maps = [dict(x1T=res0.results[c]["x1T"],
                        **{k: per_core[c][k] for k in _L1_KEYS})
                   for c in range(8)]
        res1 = run_bass_kernel_spmd(nc1, in_maps, core_ids=list(range(8)))
        for c in range(8):
            b, par = c >> 1, c & 1
            out[b, par * 2048:(par + 1) * 2048, :] = res1.results[c]["outT"].T
    return out


# version 04
# speedup vs baseline: 1.0858x; 1.0858x over previous
import numpy as np
import ml_dtypes
import concourse.bass as bass
import concourse.mybir as mybir
from concourse.bass_utils import run_bass_kernel_spmd

F32 = mybir.dt.float32
BF16 = mybir.dt.bfloat16
AF = mybir.ActivationFunctionType
ALU = mybir.AluOpType
BF16NP = ml_dtypes.bfloat16

E = 2048
S = 4096
NB = 4
NHL = 8
HD = 128
INTER = 8192
NFC = 16
NIC = 64
TT = 512
NT_FULL = 8
NT_OWN = 4
RS2048 = float(1.0 / np.sqrt(2048.0))
RS8192 = float(1.0 / np.sqrt(8192.0))
SMS = float(1.0 / np.sqrt(128.0))
EPS = 1e-5


def _had(n):
    i = np.arange(n)[:, None]
    j = np.arange(n)[None, :]
    return ((-1.0) ** np.bitwise_count(i & j)).astype(np.float32)


def pack_lhsT(w):
    K, M = w.shape
    return np.ascontiguousarray(
        w.reshape(K // 128, 128, M // 128, 128).transpose(2, 1, 0, 3))


def pack_rhs(w):
    K, M = w.shape
    return np.ascontiguousarray(w.reshape(K // 128, 128, M))


def _quads(nbits_total, sp):
    n = 1 << nbits_total
    b0, b1 = 1 << (2 * sp), 1 << (2 * sp + 1)
    out = []
    for c in range(n):
        if (c & b0) == 0 and (c & b1) == 0:
            out.append((c, c + b0, c + b1, c + b0 + b1))
    return out


class Prog:

    def __init__(self, nc, layers, T, C, SB, PS, SEM):
        self.nc = nc
        self.layers = layers
        self.T = T
        self.C = C
        self.SB = SB
        self.PS = PS
        self.SEM = SEM

    def start(self, role, eng):
        self.role = role
        self.eng = eng
        self.stoplim = getattr(self, "stoplim", None)
        self.skipset = getattr(self, "skipset", None)
        self.c = {k: 0 for k in ("qld", "qst", "qgp", "cc", "spe", "sact",
                                 "sdve", "qp0", "qp1", "qs0", "qs1",
                                 "qq0", "qq1", "qt0", "qt1", "bpe", "bact",
                                 "bdve", "bpool", "bsync", "phase")}
        self.lastwait = {}
        self.lu = {}

    def on(self, *roles):
        return self.role in roles

    def _wait(self, sem, val):
        if val <= 0:
            return
        key = sem
        if self.lastwait.get(key, -1) >= val:
            return
        self.lastwait[key] = val
        self.eng.wait_ge(self.SEM[sem], val)

    def w(self, role, *sems):
        if self.on(role):
            for s in sems:
                self._wait(s, self.c[s])

    def wv(self, role, sem, val):
        if self.on(role):
            self._wait(sem, val)

    def dma(self, q, out, in_, sem=None):
        qsem = sem or {"sync": "qld", "scalar": "qst", "pool": "qgp"}[q]
        qrole = {"sync": "sync", "scalar": "act", "pool": "pool"}[q]
        if self.on(qrole):
            self.eng.dma_start(out=out, in_=in_).then_inc(self.SEM[qsem], 16)
        self.c[qsem] += 16

    def mm(self, out, lhsT, rhs, start, stop, inc=False, skip=False):
        if self.on("pe"):
            i = self.eng.matmul(out, lhsT, rhs, start=start, stop=stop,
                                skip_group_check=skip)
            if inc:
                i.then_inc(self.SEM["spe"], 1)
        if inc:
            self.c["spe"] += 1

    def op(self, role, fn, inc=None):
        auto = {"dve": "sdve", "act": "sact"}.get(role)
        if self.on(role):
            if auto:
                self._wait(auto, self.c[auto])
            i = fn(self.eng)
            if auto:
                i.then_inc(self.SEM[auto], 1)
            if inc and inc != auto:
                i.then_inc(self.SEM[inc], 1)
        if auto:
            self.c[auto] += 1
        if inc and inc != auto:
            self.c[inc] += 1

    def rbank(self, key, sem):
        self.lu[key] = (sem, self.c[sem])

    def wbank(self, role, key):
        pu = self.lu.get(key)
        if pu:
            self.wv(role, *pu)

    def barrier(self):
        self.c["phase"] += 1
        ph = self.c["phase"]
        mine = {"pe": "bpe", "act": "bact", "dve": "bdve",
                "pool": "bpool", "sync": "bsync"}[self.role] \
            if self.role in ("pe", "act", "dve", "pool", "sync") else None
        for s in ("qld", "qst", "qgp", "qp0", "qp1", "qs0", "qs1",
                  "qq0", "qq1", "qt0", "qt1", "cc"):
            self._wait(s, self.c[s])
        if self.role in ("pe", "act", "dve", "pool"):
            self.eng.drain()
        self.eng.sem_inc(self.SEM[mine], 1)
        for b in ("bpe", "bact", "bdve", "bpool", "bsync"):
            self.c[b] = ph
            if b != mine:
                self._wait(b, ph)

    def chk(self):
        self.step += 1
        if self.skipset and self.step in self.skipset:
            return False
        return self.stoplim is None or self.step <= self.stoplim

    def program(self):
        self.step = 0
        self.stopped = False
        self.ph_setup()
        self.barrier()
        for L in self.layers:
            if L == 0:
                if self.chk():
                    self.ph_ln(0, "full")
                self.barrier()
                if self.chk():
                    self.ph_ln(0, "own")
            else:
                self.chk()
                if self.chk():
                    self.ph_ln(1, "own")
                    self.w("pool", "qst")
                    self.op("pool", lambda e: e.collective_compute(
                        "AllGather", ALU.bypass,
                        replica_groups=[[0, 1], [2, 3], [4, 5], [6, 7]],
                        ins=[self.T["xn1_own_cc"][:, :]],
                        outs=[self.T["xn1_full_cc"][:, :]]), inc="cc")
            self.barrier()
            if self.chk():
                self.ph_ffn1(L)
            self.barrier()
            if self.chk():
                self.ph_ffn2(L)
            self.barrier()
            if self.chk():
                self.ph_qkv(L)
            self.barrier()
            if self.chk():
                self.ph_bounds(L)
            self.barrier()
            if self.chk():
                self.ph_attn(L)
            self.barrier()
            if self.chk():
                self.ph_outproj(L)
            self.barrier()
            if self.chk():
                self.ph_resid(L)
            self.barrier()

    def ph_setup(self):
        T, C, SB = self.T, self.C, self.SB
        self.dma("sync", SB["h128"][:, :], C["h128"])
        self.dma("sync", SB["ones"][:, :], C["ones"])
        for o in range(4):
            self.dma("sync", SB["mask"][:, o, :], C["mask"][o])
        if 0 in self.layers:
            self.dma("sync", SB["su"][:, :], T["su_ug_t"])
            self.dma("sync", SB["mu"][:, 0:128], T["wsc_ug_t"])
            self.dma("sync", SB["mu"][:, 128:192], T["su_dp_t"])
            self.dma("sync", SB["mu"][:, 192:208], T["wsc_dp_t"])
            self.dma("sync", SB["bias"][:, 0:8], T["bq0_t"])
            self.dma("sync", SB["bias"][:, 8:9], T["bk0_t"])
            self.dma("sync", SB["bias"][:, 9:25], T["bfc_t"])
            self.dma("pool", SB["bvrow"][:, :], T["bv0_row"])
        self.w("dve", "qld", "qgp")
        A = SB["alpha"]
        W = SB["mu"]
        if 0 in self.layers:
            self.op("dve", lambda e: e.tensor_mul(A[:, 0:64], W[:, 0:64],
                                                  W[:, 128:192]))
            self.op("dve", lambda e: e.tensor_scalar_mul(A[:, 0:64], A[:, 0:64],
                                                         RS2048))
            self.op("dve", lambda e: e.tensor_scalar_mul(A[:, 64:128],
                                                         W[:, 64:128], RS2048))
            self.op("dve", lambda e: e.tensor_scalar_mul(A[:, 128:144],
                                                         W[:, 192:208], RS8192))
        self.op("dve", lambda e: e.memset(A[:, 144:145], 1.0))
        self.op("dve", lambda e: e.memset(self.SB["nt"][:, 6:7], EPS))
        self.op("dve", lambda e: e.memset(SB["bias"][:, 25:26], 0.0))

    def ph_ln(self, L, kind):
        T, SB, PS = self.T, self.SB, self.PS
        if L == 0 and kind == "full":
            ntiles, fwht, to_act = NT_FULL, False, False
            src = lambda t, fc: T["x0T_full"][fc * 128:(fc + 1) * 128,
                                              t * TT:(t + 1) * TT]
            dst = lambda t, fc: T["xnT_hbm"][fc * 128:(fc + 1) * 128,
                                             t * TT:(t + 1) * TT]
        elif L == 0:
            ntiles, fwht, to_act = NT_OWN, True, True
            src = lambda t, fc: T["x0T_own"][fc * 128:(fc + 1) * 128,
                                             t * TT:(t + 1) * TT]
            dst = None
        else:
            ntiles, fwht, to_act = NT_OWN, False, True
            src = lambda t, fc: T["x1T"][fc * 128:(fc + 1) * 128,
                                         t * TT:(t + 1) * TT]
            dst = lambda t, fc: T["xn1_own_cc"][fc * 128:(fc + 1) * 128,
                                                t * TT:(t + 1) * TT]
        AB = SB["actbuf"]
        for t in range(ntiles):
            self.w("sync", "sdve")
            for fc in range(NFC):
                self.dma("sync", SB["res32"][:, fc, :], src(t, fc))
            for fc in range(NFC):
                sl = fc % 2
                if fc == 0:
                    self.w("dve", "qld")
                self.w("dve", "spe")
                self.op("dve", lambda e, fc=fc, sl=sl: e.tensor_copy(
                    SB["xcb"][:, sl, :], SB["res32"][:, fc, :]))
                self.op("dve", lambda e, sl=sl: e.tensor_mul(
                    SB["sqb"][:, sl, :], SB["xcb"][:, sl, :],
                    SB["xcb"][:, sl, :]))
                self.w("pe", "sdve")
                if fc == 0:
                    self.w("pe", "sact")
                self.mm(PS[0][:, :], SB["ones"][:, :], SB["xcb"][:, sl, :],
                        start=(fc == 0), stop=(fc == NFC - 1))
                self.mm(PS[1][:, :], SB["ones"][:, :], SB["sqb"][:, sl, :],
                        start=(fc == 0), stop=(fc == NFC - 1),
                        inc=True)
            self.w("dve", "spe")
            self.op("dve", lambda e: e.tensor_scalar_mul(SB["mu"][:, :],
                                                         PS[0][:, :], 1.0 / E))
            self.op("dve", lambda e: e.tensor_scalar_mul(SB["var"][:, :],
                                                         PS[1][:, :], 1.0 / E))
            self.op("dve", lambda e: e.tensor_mul(SB["lnt"][:, :],
                                                  SB["mu"][:, :],
                                                  SB["mu"][:, :]))
            self.op("dve", lambda e: e.tensor_sub(SB["var"][:, :],
                                                  SB["var"][:, :],
                                                  SB["lnt"][:, :]))
            self.w("act", "sdve")
            self.op("act", lambda e: e.activation(SB["lnt"][:, :],
                                                  SB["var"][:, :], AF.Ln,
                                                  bias=SB["nt"][:, 6:7]))
            self.op("act", lambda e: e.activation(SB["rstd"][:, :],
                                                  SB["lnt"][:, :], AF.Exp,
                                                  scale=-0.5))
            self.w("dve", "sact", "qst")
            for fc in range(NFC):
                sl = fc % 2
                self.op("dve", lambda e, fc=fc, sl=sl: e.tensor_sub(
                    SB["xf"][:, 0, :], SB["res32"][:, fc, :], SB["mu"][:, :]))
                self.op("dve", lambda e, fc=fc, sl=sl: e.tensor_mul(
                    SB["xnb"][:, fc, :], SB["xf"][:, 0, :], SB["rstd"][:, :]))
                if to_act:
                    dstap = AB[:, fc * 2048 + t * TT: fc * 2048 + t * TT + TT]
                    if L == 0:
                        self.op("dve", lambda e, fc=fc, d=dstap: e.tensor_scalar_mul(
                            d, SB["xnb"][:, fc, :], SB["su"][:, fc:fc + 1]))
                    else:
                        self.op("dve", lambda e, fc=fc, d=dstap: e.tensor_copy(
                            d, SB["xnb"][:, fc, :]))
            self.op("dve", lambda e: e.engine_nop())
            if dst is not None:
                self.w("act", "sdve")
                for fc in range(NFC):
                    self.dma("scalar", dst(t, fc), SB["xnb"][:, fc, :])
            if fwht:
                A = lambda cc: AB[:, cc * 2048 + t * TT: cc * 2048 + t * TT + TT]
                BT = SB["btmp"]
                for sp in range(2):
                    for (c0, c1, c2, c3) in _quads(4, sp):
                        self.op("dve", lambda e, a=A(c0), b=A(c1): e.tensor_add(BT[:, 0, :], a, b))
                        self.op("dve", lambda e, a=A(c0), b=A(c1): e.tensor_sub(BT[:, 1, :], a, b))
                        self.op("dve", lambda e, a=A(c2), b=A(c3): e.tensor_add(BT[:, 2, :], a, b))
                        self.op("dve", lambda e, a=A(c2), b=A(c3): e.tensor_sub(BT[:, 3, :], a, b))
                        self.op("dve", lambda e, o=A(c0): e.tensor_add(o, BT[:, 0, :], BT[:, 2, :]))
                        self.op("dve", lambda e, o=A(c2): e.tensor_sub(o, BT[:, 0, :], BT[:, 2, :]))
                        self.op("dve", lambda e, o=A(c1): e.tensor_add(o, BT[:, 1, :], BT[:, 3, :]))
                        self.op("dve", lambda e, o=A(c3): e.tensor_sub(o, BT[:, 1, :], BT[:, 3, :]))
                self.op("dve", lambda e: e.engine_nop())
                for fc in range(NFC):
                    pb = 2 + (fc % 2)
                    self.w("pe", "sdve")
                    self.mm(self.PS[pb][:, :], SB["h128"][:, :], A(fc),
                            start=True, stop=True, inc=True)
                    self.w("dve", "spe")
                    self.op("dve", lambda e, fc=fc, pb=pb: e.tensor_copy(
                        A(fc), self.PS[pb][:, :]))

    def ph_ffn1(self, L):
        T, SB, PS = self.T, self.SB, self.PS
        AB = SB["actbuf"]
        wup = T["wup_pk"][L]
        wgt = T["wgate_pk"][L]
        for ob in range(NIC):
            sl = ob % 2
            pu = self.lu.get(("wblk", sl))
            if pu:
                self.wv("pool", *pu)
            qp = f"qp{sl}"
            self.dma("pool", SB["wblk"][:, sl, :, :], wup[ob], sem=qp)
            self.dma("pool", SB["wblk"][:, 2 + sl, :, :], wgt[ob], sem=qp)
            ldc = self.c[qp]
            for ts in range(4):
                bg = (ob * 4 + ts) % 2
                bu = 2 + (ob * 4 + ts) % 2
                if ts == 0:
                    self.wv("pe", qp, ldc)
                self.wbank("pe", ("gb", bg))
                for kc in range(NFC):
                    self.mm(PS[bg][:, :], SB["wblk"][:, 2 + sl, kc, :],
                            AB[:, kc * 2048 + ts * TT: kc * 2048 + ts * TT + TT],
                            start=(kc == 0), stop=(kc == NFC - 1),
                            inc=(kc == NFC - 1))
                self.wbank("act", ("sl", bg))
                self.w("act", "spe")
                gsc = SB["alpha"][:, 64 + ob:64 + ob + 1] if L == 0 else 1.0
                self.op("act", lambda e, bg=bg, gsc=gsc: e.activation(
                    SB["silu"][:, (0 if bg == 0 else 1), :], PS[bg][:, :],
                    AF.Silu, scale=gsc))
                self.rbank(("gb", bg), "sact")
                self.wbank("pe", ("ub", bu))
                for kc in range(NFC):
                    self.mm(PS[bu][:, :], SB["wblk"][:, sl, kc, :],
                            AB[:, kc * 2048 + ts * TT: kc * 2048 + ts * TT + TT],
                            start=(kc == 0), stop=(kc == NFC - 1),
                            inc=(kc == NFC - 1))
                ysl = (ob * 4 + ts) % 2
                qt = f"qt{ysl}"
                self.wbank("dve", ("yb", ysl))
                self.w("dve", "spe", "sact")
                usc = SB["alpha"][:, ob:ob + 1] if L == 0 \
                    else SB["alpha"][:, 144:145]
                self.op("dve", lambda e, bu=bu, bg=bg, ysl=ysl, usc=usc:
                        e.scalar_tensor_tensor(
                            SB["ybuf"][:, ysl, :], PS[bu][:, :], usc,
                            SB["silu"][:, (0 if bg == 0 else 1), :],
                            ALU.mult, ALU.mult))
                self.rbank(("ub", bu), "sdve")
                self.rbank(("sl", bg), "sdve")
                self.w("act", "sdve")
                self.dma("scalar", T["yH"][ob][:, ts * TT:(ts + 1) * TT],
                         SB["ybuf"][:, ysl, :], sem=qt)
                self.rbank(("yb", ysl), qt)
            self.lu[("wblk", sl)] = ("spe", self.c["spe"])

    def ph_ffn2(self, L):
        T, SB, PS = self.T, self.SB, self.PS
        AB = SB["actbuf"]
        wdn = T["wdn_pk"][L]
        res_src = (lambda t, fc: T["x0T_own"][fc * 128:(fc + 1) * 128,
                                              t * TT:(t + 1) * TT]) if L == 0 \
            else (lambda t, fc: T["x1T"][fc * 128:(fc + 1) * 128,
                                         t * TT:(t + 1) * TT])
        for t in range(NT_OWN):
            self.w("sync", "spe", "sdve")
            for ic in range(NIC):
                self.dma("sync", AB[:, ic * TT:(ic + 1) * TT],
                         T["yH"][ic][:, t * TT:(t + 1) * TT])
            for fc in range(NFC):
                self.dma("sync", SB["res32"][:, fc, :], res_src(t, fc))
            self.w("dve", "qld")
            A = lambda cc: AB[:, cc * TT:(cc + 1) * TT]
            if L == 0:
                BT = SB["btmp"]
                for sp in range(3):
                    for (c0, c1, c2, c3) in _quads(6, sp):
                        self.op("dve", lambda e, a=A(c0), b=A(c1): e.tensor_add(BT[:, 0, :], a, b))
                        self.op("dve", lambda e, a=A(c0), b=A(c1): e.tensor_sub(BT[:, 1, :], a, b))
                        self.op("dve", lambda e, a=A(c2), b=A(c3): e.tensor_add(BT[:, 2, :], a, b))
                        self.op("dve", lambda e, a=A(c2), b=A(c3): e.tensor_sub(BT[:, 3, :], a, b))
                        self.op("dve", lambda e, o=A(c0): e.tensor_add(o, BT[:, 0, :], BT[:, 2, :]))
                        self.op("dve", lambda e, o=A(c2): e.tensor_sub(o, BT[:, 0, :], BT[:, 2, :]))
                        self.op("dve", lambda e, o=A(c1): e.tensor_add(o, BT[:, 1, :], BT[:, 3, :]))
                        self.op("dve", lambda e, o=A(c3): e.tensor_sub(o, BT[:, 1, :], BT[:, 3, :]))
                self.op("dve", lambda e: e.engine_nop())
                bfly_done = self.c["sdve"]
                for ic in range(NIC):
                    pb = 2 + (ic % 2)
                    if ic == 0:
                        self.wv("pe", "sdve", bfly_done)
                    self.wbank("pe", ("hb", pb))
                    self.mm(PS[pb][:, :], SB["h128"][:, :], A(ic),
                            start=True, stop=True, inc=True)
                    self.w("dve", "spe")
                    self.op("dve", lambda e, ic=ic, pb=pb: e.tensor_copy(
                        A(ic), PS[pb][:, :]))
                    self.rbank(("hb", pb), "sdve")
                self.w("pe", "sdve")
            self.w("pe", "qld")
            for mb in range(NFC):
                sl = mb % 2
                pu = self.lu.get(("wdnb", sl))
                if pu:
                    self.wv("pool", *pu)
                off = sl * 8192
                qp = f"qp{sl}"
                self.dma("pool", SB["buf32k"][:, off:off + 8192],
                         wdn[mb].rearrange("p a b -> p (a b)"), sem=qp)
                ldc = self.c[qp]
                pb = 4 + (mb % 2)
                self.wv("pe", qp, ldc)
                self.wbank("pe", ("db", pb))
                for ic in range(NIC):
                    self.mm(PS[pb][:, :],
                            SB["buf32k"][:, off + ic * 128: off + (ic + 1) * 128],
                            A(ic), start=(ic == 0), stop=(ic == NIC - 1),
                            inc=(ic == NIC - 1))
                self.lu[("wdnb", sl)] = ("spe", self.c["spe"])
                dsc = SB["alpha"][:, 128 + mb:128 + mb + 1] if L == 0 \
                    else SB["alpha"][:, 144:145]
                osl = mb % 2
                qt = f"qt{osl}"
                self.wbank("dve", ("o32", osl))
                self.w("dve", "spe")
                self.op("dve", lambda e, pb=pb, mb=mb, osl=osl, dsc=dsc:
                        e.scalar_tensor_tensor(
                            SB["o32"][:, osl, :], PS[pb][:, :], dsc,
                            SB["res32"][:, mb, :], ALU.mult, ALU.add))
                self.rbank(("db", pb), "sdve")
                self.w("act", "sdve")
                self.dma("scalar",
                         T["x1pT"][mb * 128:(mb + 1) * 128, t * TT:(t + 1) * TT],
                         SB["o32"][:, osl, :], sem=qt)
                self.rbank(("o32", osl), qt)

    def ph_qkv(self, L):
        T, SB, PS = self.T, self.SB, self.PS
        AB = SB["actbuf"]
        DV = 128 if L == 0 else 1024
        nkh = 1 if L == 0 else NHL
        for kc in range(NFC):
            self.dma("pool", SB["buf32k"][:, kc * DV:(kc + 1) * DV],
                     T["wv_rhs"][L][kc])
        for th in range(2):
            self.w("sync", "spe")
            if L == 1 and th == 0:
                self.wv("sync", "cc", self.c["cc"])
            for fc in range(NFC):
                if L == 0:
                    srcap = T["xnT_hbm"][fc * 128:(fc + 1) * 128,
                                         th * 2048:(th + 1) * 2048]
                else:
                    srcap = T["xn1_full_cc"][th * 2048 + fc * 128:
                                             th * 2048 + (fc + 1) * 128, :]
                self.dma("sync", AB[:, fc * 2048:(fc + 1) * 2048], srcap)
            ldx = self.c["qld"]
            for (nmb, wpk, dstT, bias) in (
                (NHL, T["wq_pk"][L], T["qT_hbm"],
                 SB["bias"][:, 0:8] if L == 0 else None),
                (nkh, T["wk_pk"][L], T["kT_hbm"],
                 SB["bias"][:, 8:9] if L == 0 else None),
            ):
                for mb in range(nmb):
                    sl = mb % 2
                    pu = self.lu.get(("wblk", sl))
                    if pu:
                        self.wv("pool", *pu)
                    qp = f"qp{sl}"
                    self.dma("pool", SB["wblk"][:, sl, :, :], wpk[mb], sem=qp)
                    ldc = self.c[qp]
                    for tq in range(4):
                        pb = tq % 2
                        if tq == 0:
                            self.wv("pe", qp, ldc)
                            self.wv("pe", "qld", ldx)
                        self.wbank("pe", ("qb", pb))
                        for kc in range(NFC):
                            self.mm(PS[pb][:, :], SB["wblk"][:, sl, kc, :],
                                    AB[:, kc * 2048 + tq * TT:
                                       kc * 2048 + tq * TT + TT],
                                    start=(kc == 0), stop=(kc == NFC - 1),
                                    inc=(kc == NFC - 1))
                        esl = tq % 2
                        qt = f"qt{esl}"
                        self.wbank("dve", ("qev", esl))
                        self.w("dve", "spe")
                        if bias is not None:
                            bcol = bias[:, mb:mb + 1]
                            self.op("dve", lambda e, pb=pb, esl=esl, b=bcol:
                                    e.tensor_scalar_add(SB["qev"][:, esl, :],
                                                        PS[pb][:, :], b))
                        else:
                            self.op("dve", lambda e, pb=pb, esl=esl:
                                    e.tensor_copy(SB["qev"][:, esl, :],
                                                  PS[pb][:, :]))
                        self.rbank(("qb", pb), "sdve")
                        self.w("act", "sdve")
                        self.dma("scalar",
                                 dstT[mb * 128:(mb + 1) * 128,
                                      th * 2048 + tq * TT:
                                      th * 2048 + (tq + 1) * TT],
                                 SB["qev"][:, esl, :], sem=qt)
                        self.rbank(("qev", esl), qt)
                    self.lu[("wblk", sl)] = ("spe", self.c["spe"])
            for tsub in range(16):
                for dvg in range(DV // 512 if DV >= 512 else 1):
                    pb = 4 + (tsub % 2)
                    n = min(DV, 512)
                    if tsub == 0 and dvg == 0:
                        self.w("pe", "qgp")
                    self.wbank("pe", ("vb", pb))
                    if L == 0:
                        self.mm(PS[pb][:, 0:128], SB["ones"][0:1, :],
                                SB["bvrow"][:, :], start=True, stop=False,
                                skip=True)
                    for kc in range(NFC):
                        self.mm(PS[pb][:, 0:n],
                                AB[:, kc * 2048 + tsub * 128:
                                   kc * 2048 + tsub * 128 + 128],
                                SB["buf32k"][:, kc * DV + dvg * 512:
                                             kc * DV + dvg * 512 + n],
                                start=(kc == 0 and L == 1), stop=(kc == NFC - 1),
                                inc=(kc == NFC - 1), skip=(L == 0))
                    esl = tsub % 2
                    qt = f"qt{esl}"
                    self.wbank("dve", ("qev", esl))
                    self.w("dve", "spe")
                    self.op("dve", lambda e, pb=pb, esl=esl, n=n: e.tensor_copy(
                        SB["qev"][:, esl, 0:n], PS[pb][:, 0:n]))
                    self.rbank(("vb", pb), "sdve")
                    self.w("act", "sdve")
                    for hj in range(n // 128):
                        head = dvg * 4 + hj
                        tc = th * 16 + tsub
                        self.dma("scalar", T["v_hbm"][head, tc],
                                 SB["qev"][:, esl, hj * 128:(hj + 1) * 128],
                                 sem=qt)
                    self.rbank(("qev", esl), qt)

    def ph_bounds(self, L):
        T, SB, PS = self.T, self.SB, self.PS
        nkh = 1 if L == 0 else NHL
        for (nh, srcT, dst) in ((NHL, T["qT_hbm"], SB["mq2"]),
                                (nkh, T["kT_hbm"], SB["mk2"])):
            for h in range(nh):
                self.w("sync", "sdve")
                self.dma("sync", SB["buf32k"][:, 0:4096],
                         srcT[h * 128:(h + 1) * 128, :])
                self.w("dve", "qld")
                for g in range(8):
                    sl = g % 2
                    self.w("dve", "spe")
                    self.op("dve", lambda e, g=g, sl=sl: e.tensor_mul(
                        SB["sqb"][:, sl, :],
                        SB["buf32k"][:, g * TT:(g + 1) * TT],
                        SB["buf32k"][:, g * TT:(g + 1) * TT]))
                    pb = g % 2
                    self.w("pe", "sdve")
                    self.w("pe", "sact")
                    self.mm(PS[pb][:, :], SB["ones"][:, :], SB["sqb"][:, sl, :],
                            start=True, stop=True, inc=True)
                    self.w("dve", "spe")
                    self.op("dve", lambda e, pb=pb: e.reduce_max(
                        SB["nt"][:, 0:1], PS[pb][:, :], mybir.AxisListType.X))
                    if g == 0:
                        self.op("dve", lambda e, h=h, dst=dst: e.tensor_copy(
                            dst[:, h:h + 1], SB["nt"][:, 0:1]))
                    else:
                        self.op("dve", lambda e, h=h, dst=dst: e.tensor_max(
                            dst[:, h:h + 1], dst[:, h:h + 1], SB["nt"][:, 0:1]))
        self.w("act", "sdve")
        for h in range(NHL):
            kh = 0 if L == 0 else h
            self.op("act", lambda e, h=h: e.activation(
                SB["nt"][:, 1:2], SB["mq2"][:, h:h + 1], AF.Ln))
            self.op("act", lambda e, kh=kh: e.activation(
                SB["nt"][:, 2:3], SB["mk2"][:, kh:kh + 1], AF.Ln))
            self.w("dve", "sact")
            self.op("dve", lambda e: e.tensor_add(
                SB["nt"][:, 3:4], SB["nt"][:, 1:2], SB["nt"][:, 2:3]))
            self.w("act", "sdve")
            self.op("act", lambda e: e.activation(
                SB["nt"][:, 4:5], SB["nt"][:, 3:4], AF.Exp, scale=0.5))
            self.w("dve", "sact")
            self.op("dve", lambda e, h=h: e.tensor_scalar_mul(
                SB["bh"][:, h:h + 1], SB["nt"][:, 4:5], -SMS))

    def ph_attn(self, L):
        T, SB, PS = self.T, self.SB, self.PS
        for h in range(NHL):
            kv = 0 if L == 0 else h
            ksl = h % 2
            koff = ksl * 4096
            voff = 8192 + ksl * 4096
            qs = f"qs{ksl}"
            if not (L == 0 and h > 0):
                pu = self.lu.get(("kvs", ksl))
                if pu:
                    self.wv("sync", *pu)
                self.dma("sync", SB["buf32k"][:, koff:koff + 4096],
                         T["kT_hbm"][kv * 128:(kv + 1) * 128, :], sem=qs)
                for tc in range(32):
                    self.dma("sync", SB["buf32k"][:, voff + tc * 128:
                                                  voff + (tc + 1) * 128],
                             T["v_hbm"][kv, tc], sem=qs)
            if L == 0 and h > 0:
                koff = 0
                voff = 8192
                ksl = 0
                qs = "qs0"
            ldk = self.c[qs]
            for g in range(8):
                qsl = g % 2
                qq = f"qq{qsl}"
                self.w("sync", "spe")
                self.dma("sync", SB["qg"][:, qsl, :],
                         T["qT_hbm"][h * 128:(h + 1) * 128,
                                     g * TT:(g + 1) * TT], sem=qq)
                ldq = self.c[qq]
                nkc = 4 * g + 4
                bs_ = 2 + (g % 2)
                ba_ = 4 + (g % 2)
                for kc in range(nkc):
                    sb = kc % 2
                    if kc == 0:
                        self.wv("pe", qq, ldq)
                        self.wv("pe", qs, ldk)
                        self.wbank("pe", ("sumb", g % 2))
                        self.wbank("pe", ("avb", g % 2))
                    self.wbank("pe", ("scb", sb))
                    self.mm(PS[sb][:, :],
                            SB["buf32k"][:, koff + kc * 128:koff + (kc + 1) * 128],
                            SB["qg"][:, qsl, :], start=True, stop=True,
                            inc=True)
                    psl = kc % 3
                    self.wbank("act", ("pslot", psl))
                    self.w("act", "spe")
                    self.op("act", lambda e, sb=sb, psl=psl, h=h: e.activation(
                        SB["pbuf"][:, psl, :], PS[sb][:, :], AF.Exp,
                        bias=SB["bh"][:, h:h + 1], scale=SMS))
                    self.rbank(("scb", sb), "sact")
                    if kc >= 4 * g:
                        o = kc - 4 * g
                        msl = kc % 2
                        self.wbank("dve", ("pm", msl))
                        self.w("dve", "sact")
                        self.op("dve", lambda e, psl=psl, msl=msl, o=o:
                                e.tensor_mul(SB["pmask"][:, msl, :],
                                             SB["pbuf"][:, psl, :],
                                             SB["mask"][:, o, :]))
                        self.rbank(("pslot", psl), "sdve")
                        used = SB["pmask"][:, msl, :]
                        self.w("pe", "sdve")
                    else:
                        used = SB["pbuf"][:, psl, :]
                        self.w("pe", "sact")
                    self.mm(PS[bs_][:, :], SB["ones"][:, :], used,
                            start=(kc == 0), stop=(kc == nkc - 1))
                    self.mm(PS[ba_][:, :],
                            SB["buf32k"][:, voff + kc * 128:voff + (kc + 1) * 128],
                            used, start=(kc == 0), stop=(kc == nkc - 1),
                            inc=True)
                    if kc >= 4 * g:
                        self.rbank(("pm", kc % 2), "spe")
                    else:
                        self.rbank(("pslot", psl), "spe")
                self.wbank("act", ("rsum",))
                self.w("act", "spe")
                self.op("act", lambda e, bs_=bs_: e.activation(
                    SB["lnt"][:, :], PS[bs_][:, :], AF.Ln))
                self.op("act", lambda e: e.activation(
                    SB["rsum"][:, 0, :], SB["lnt"][:, :], AF.Exp,
                    scale=-1.0))
                self.rbank(("sumb", g % 2), "sact")
                self.wbank("dve", ("aev", g % 2))
                self.w("dve", "sact")
                self.op("dve", lambda e, ba_=ba_, g=g: e.tensor_mul(
                    SB["attn_ev"][:, g % 2, :], PS[ba_][:, :],
                    SB["rsum"][:, 0, :]))
                self.rbank(("avb", g % 2), "sdve")
                self.rbank(("rsum",), "sdve")
                self.w("act", "sdve")
                qt = f"qt{g % 2}"
                self.dma("scalar",
                         T["attnT_hbm"][h * 128:(h + 1) * 128,
                                        g * TT:(g + 1) * TT],
                         SB["attn_ev"][:, g % 2, :], sem=qt)
                self.rbank(("aev", g % 2), qt)
            self.lu[("kvs", h % 2)] = ("spe", self.c["spe"])

    def ph_outproj(self, L):
        T, SB, PS = self.T, self.SB, self.PS
        AB = SB["actbuf"]
        wo = T["wo_pk"][L]
        for hc in range(NHL):
            self.dma("sync", AB[:, hc * 4096:(hc + 1) * 4096],
                     T["attnT_hbm"][hc * 128:(hc + 1) * 128, :])
        for mb in range(NFC):
            sl = mb % 2
            pu = self.lu.get(("wblk", sl))
            if pu:
                self.wv("pool", *pu)
            qp = f"qp{sl}"
            self.dma("pool", SB["wblk"][:, sl, 0:8, :], wo[mb], sem=qp)
            ldc = self.c[qp]
            for t8 in range(8):
                pb = t8 % 2
                if t8 == 0:
                    self.wv("pe", qp, ldc)
                    self.w("pe", "qld")
                self.wbank("pe", ("qb", pb))
                for kc in range(NHL):
                    self.mm(PS[pb][:, :], SB["wblk"][:, sl, kc, :],
                            AB[:, kc * 4096 + t8 * TT:
                               kc * 4096 + t8 * TT + TT],
                            start=(kc == 0), stop=(kc == NHL - 1),
                            inc=(kc == NHL - 1))
                esl = t8 % 2
                qt = f"qt{esl}"
                self.wbank("dve", ("qev", esl))
                self.w("dve", "spe")
                self.op("dve", lambda e, pb=pb, esl=esl: e.tensor_copy(
                    SB["qev"][:, esl, :], PS[pb][:, :]))
                self.rbank(("qb", pb), "sdve")
                self.w("act", "sdve")
                self.dma("scalar",
                         T["rs_in"][t8 // 4, mb * 128:(mb + 1) * 128,
                                    (t8 % 4) * TT:(t8 % 4 + 1) * TT],
                         SB["qev"][:, esl, :], sem=qt)
                self.rbank(("qev", esl), qt)
            self.lu[("wblk", sl)] = ("spe", self.c["spe"])
        self.w("pool", "qst", "qt0", "qt1")
        self.op("pool", lambda e: e.collective_compute(
            "ReduceScatter", ALU.add,
            replica_groups=[[0, 1], [2, 3], [4, 5], [6, 7]],
            ins=[T["rs_in"][:, :, :]], outs=[T["rs_out"][:, :]]), inc="cc")

    def ph_resid(self, L):
        T, SB = self.T, self.SB
        dstT = T["x1T"] if L == 0 else T["outT"]
        self.wv("sync", "cc", self.c["cc"])
        for t in range(NT_OWN):
            self.w("sync", "sdve")
            for fc in range(NFC):
                self.dma("sync", SB["xnb"][:, fc, :],
                         T["rs_out"][fc * 128:(fc + 1) * 128,
                                     t * TT:(t + 1) * TT])
                self.dma("sync", SB["res32"][:, fc, :],
                         T["x1pT"][fc * 128:(fc + 1) * 128,
                                   t * TT:(t + 1) * TT])
            self.w("dve", "qld")
            for fc in range(NFC):
                bcol = SB["bias"][:, 9 + fc:10 + fc] if L == 0 \
                    else SB["bias"][:, 25:26]
                osl = fc % 2
                qt = f"qt{osl}"
                self.wbank("dve", ("o32", osl))
                self.op("dve", lambda e, fc=fc, osl=osl, b=bcol:
                        e.scalar_tensor_tensor(
                            SB["o32"][:, osl, :], SB["xnb"][:, fc, :], b,
                            SB["res32"][:, fc, :], ALU.add, ALU.add))
                self.w("act", "sdve")
                self.dma("scalar",
                         dstT[fc * 128:(fc + 1) * 128, t * TT:(t + 1) * TT],
                         SB["o32"][:, osl, :], sem=qt)
                self.rbank(("o32", osl), qt)



def build_program(mode, taps=(), stoplim=None, skipset=None):
    layers = {"FUSED": [0, 1], "L0": [0], "L1": [1]}[mode]
    nc = bass.Bass("TRN2", target_bir_lowering=False, debug=False,
                   dynamic_dma_scratch_size=16384)
    T = {}

    def dram(name, shape, dtype, kind="Internal"):
        if name in taps and kind == "Internal":
            kind = "ExternalOutput"
        T[name] = nc.dram_tensor(name, shape, dtype, kind=kind).ap()
        return T[name]

    if 0 in layers:
        dram("x0T_full", [E, S], F32, "ExternalInput")
        dram("x0T_own", [E, 2048], F32, "ExternalInput")
        dram("su_ug_t", [128, 16], F32, "ExternalInput")
        dram("wsc_ug_t", [128, 128], F32, "ExternalInput")
        dram("su_dp_t", [128, 64], F32, "ExternalInput")
        dram("wsc_dp_t", [128, 16], F32, "ExternalInput")
        dram("bq0_t", [128, 8], F32, "ExternalInput")
        dram("bk0_t", [128, 1], F32, "ExternalInput")
        dram("bfc_t", [128, 16], F32, "ExternalInput")
        dram("bv0_row", [1, 128], F32, "ExternalInput")
    T["wq_pk"], T["wk_pk"], T["wv_rhs"], T["wo_pk"] = {}, {}, {}, {}
    T["wup_pk"], T["wgate_pk"], T["wdn_pk"] = {}, {}, {}
    for L in layers:
        T["wq_pk"][L] = dram(f"wq{L}", [8, 128, 16, 128], F32, "ExternalInput")
        T["wk_pk"][L] = dram(f"wk{L}", [1 if L == 0 else 8, 128, 16, 128],
                             F32, "ExternalInput")
        T["wv_rhs"][L] = dram(f"wv{L}", [16, 128, 128 if L == 0 else 1024],
                              F32, "ExternalInput")
        T["wo_pk"][L] = dram(f"wo{L}", [16, 128, 8, 128], F32, "ExternalInput")
        T["wup_pk"][L] = dram(f"wup{L}", [64, 128, 16, 128], F32,
                              "ExternalInput")
        T["wgate_pk"][L] = dram(f"wgate{L}", [64, 128, 16, 128], F32,
                                "ExternalInput")
        T["wdn_pk"][L] = dram(f"wdn{L}", [16, 128, 64, 128], F32,
                              "ExternalInput")
    if mode == "L0":
        dram("x1T", [E, 2048], F32, "ExternalOutput")
    elif mode == "L1":
        dram("x1T", [E, 2048], F32, "ExternalInput")
    else:
        dram("x1T", [E, 2048], F32, "Internal")
    if 1 in layers:
        dram("outT", [E, 2048], F32, "ExternalOutput")
    dram("xnT_hbm", [E, S], BF16)
    dram("qT_hbm", [1024, S], BF16)
    dram("kT_hbm", [1024, S], BF16)
    dram("v_hbm", [8, 32, 128, 128], BF16)
    dram("attnT_hbm", [1024, S], BF16)
    dram("x1pT", [E, 2048], F32)
    dram("yH", [64, 128, 2048], BF16)
    dram("rs_in", [2, E, 2048], BF16)
    dram("rs_out", [E, 2048], BF16)
    if "dbga" in taps:
        dram("dbga", [E, 2048], BF16)
    if 1 in layers:
        dram("xn1_own_cc", [E, 2048], BF16)
        dram("xn1_full_cc", [2 * E, 2048], BF16)

    C = {}
    C["h128"] = nc.inline_tensor(_had(128).astype(BF16NP), name="h128c").ap()
    C["ones"] = nc.inline_tensor(np.ones((128, 128), np.float32).astype(BF16NP),
                                 name="onesc").ap()
    mask = np.zeros((4, 128, 512), np.float32)
    for o in range(4):
        for p in range(128):
            mask[o, p, o * 128 + p:] = 1.0
    C["mask"] = nc.inline_tensor(mask.astype(BF16NP), name="maskc").ap()

    from contextlib import ExitStack
    _sb_specs = [
        ("actbuf", [128, 32768], BF16), ("buf32k", [128, 16384], BF16),
        ("wblk", [128, 4, 16, 128], BF16), ("res32", [128, 16, 512], F32),
        ("xnb", [128, 16, 512], BF16), ("xcb", [128, 2, 512], BF16),
        ("sqb", [128, 2, 512], BF16), ("xf", [128, 1, 512], F32),
        ("btmp", [128, 4, 512], BF16), ("mu", [128, 512], F32),
        ("var", [128, 512], F32), ("lnt", [128, 512], F32),
        ("rstd", [128, 512], F32), ("silu", [128, 2, 512], BF16),
        ("ybuf", [128, 2, 512], BF16), ("o32", [128, 2, 512], F32),
        ("qev", [128, 2, 512], BF16), ("qg", [128, 2, 512], BF16),
        ("pbuf", [128, 3, 512], BF16), ("pmask", [128, 2, 512], BF16),
        ("rsum", [128, 1, 512], F32), ("attn_ev", [128, 2, 512], BF16),
        ("mask", [128, 4, 512], BF16), ("h128", [128, 128], BF16),
        ("ones", [128, 128], BF16), ("su", [128, 16], F32),
        ("alpha", [128, 160], F32),
        ("bias", [128, 26], F32), ("bvrow", [1, 128], BF16),
        ("mq2", [128, 8], F32), ("mk2", [128, 8], F32),
        ("bh", [128, 8], F32), ("nt", [128, 8], F32),
    ]
    _sem_names = ("qld", "qst", "qgp", "cc", "spe", "sact", "sdve",
                  "qp0", "qp1", "qs0", "qs1", "qq0", "qq1", "qt0", "qt1",
                  "bpe", "bact", "bdve", "bpool", "bsync")
    with ExitStack() as _es:
        SB = {n: _es.enter_context(nc.sbuf_tensor(n, sh, dt))
              for (n, sh, dt) in _sb_specs}
        PS = [_es.enter_context(nc.psum_tensor(f"p{i}", [128, 512], F32))
              for i in range(6)]
        SEM = {n: _es.enter_context(nc.semaphore(n)) for n in _sem_names}
        block = _es.enter_context(nc.Block())
        prog = Prog(nc, layers, T, C, SB, PS, SEM)
        prog.stoplim = stoplim
        prog.skipset = skipset

        @block.tensor
        def _(eng):
            prog.start("pe", eng)
            prog.program()

        @block.scalar
        def _(eng):
            prog.start("act", eng)
            prog.program()

        @block.vector
        def _(eng):
            prog.start("dve", eng)
            prog.program()

        @block.gpsimd
        def _(eng):
            prog.start("pool", eng)
            prog.program()

        @block.sync
        def _(eng):
            prog.start("sync", eng)
            prog.program()

    return nc



def _stage_inputs(ins):
    per_core = [dict() for _ in range(8)]
    shared = {}

    w_qkv = np.asarray(ins["L0_attn_w_qkv"])
    b_qkv = np.asarray(ins["L0_attn_b_qkv"])
    w_fc = np.asarray(ins["L0_attn_w_fc"])
    b_fc = np.asarray(ins["L0_attn_b_fc"])
    shared["wk0"] = pack_lhsT(w_qkv[:, 2048:2176])
    shared["wv0"] = pack_rhs(w_qkv[:, 2176:2304])
    shared["bk0_t"] = np.ascontiguousarray(b_qkv[2048:2176].reshape(128, 1))
    shared["bv0_row"] = np.ascontiguousarray(b_qkv[2176:2304].reshape(1, 128))
    shared["bfc_t"] = np.ascontiguousarray(b_fc.reshape(16, 128).T)
    shared["su_ug_t"] = np.ascontiguousarray(
        np.asarray(ins["L0_ffn_SU_ug"]).reshape(16, 128).T)
    wsc = np.asarray(ins["L0_ffn_Wscale_ug"])
    shared["wsc_ug_t"] = np.ascontiguousarray(np.concatenate(
        [wsc[:8192].reshape(64, 128).T, wsc[8192:].reshape(64, 128).T], axis=1))
    shared["su_dp_t"] = np.ascontiguousarray(
        np.asarray(ins["L0_ffn_SU_dp"]).reshape(64, 128).T)
    shared["wsc_dp_t"] = np.ascontiguousarray(
        np.asarray(ins["L0_ffn_Wscale_dp"]).reshape(16, 128).T)
    shared["wup0"] = pack_lhsT(np.asarray(ins["L0_ffn_W_up"]).T)
    shared["wgate0"] = pack_lhsT(np.asarray(ins["L0_ffn_W_gate"]).T)
    shared["wdn0"] = pack_lhsT(np.asarray(ins["L0_ffn_W_down"]).T)
    Wq, Wk, Wv, Wo = (np.asarray(ins["L1_attn_Wq"]), np.asarray(ins["L1_attn_Wk"]),
                      np.asarray(ins["L1_attn_Wv"]), np.asarray(ins["L1_attn_Wo"]))
    wgu = np.asarray(ins["L1_ffn_w_gate_up"])
    shared["wgate1"] = pack_lhsT(wgu[:8192].T)
    shared["wup1"] = pack_lhsT(wgu[8192:].T)
    shared["wdn1"] = pack_lhsT(np.asarray(ins["L1_ffn_w_down"]).T)

    x0 = np.asarray(ins["L0_attn_x"])
    for c in range(8):
        b, par = c >> 1, c & 1
        m = per_core[c]
        x0T = np.ascontiguousarray(x0[b].T)
        m["x0T_full"] = x0T
        m["x0T_own"] = np.ascontiguousarray(x0T[:, par * 2048:(par + 1) * 2048])
        hc = slice(par * 1024, (par + 1) * 1024)
        m["wq0"] = pack_lhsT(w_qkv[:, hc])
        m["wk0"] = shared["wk0"]
        m["wv0"] = shared["wv0"]
        m["wo0"] = pack_lhsT(w_fc[hc, :])
        m["bq0_t"] = np.ascontiguousarray(b_qkv[hc].reshape(8, 128).T)
        for k in ("bk0_t", "bv0_row", "bfc_t", "su_ug_t", "wsc_ug_t",
                  "su_dp_t", "wsc_dp_t"):
            m[k] = shared[k]
        m["wup0"] = shared["wup0"]
        m["wgate0"] = shared["wgate0"]
        m["wdn0"] = shared["wdn0"]
        m["wq1"] = pack_lhsT(Wq[hc, :].T)
        m["wk1"] = pack_lhsT(Wk[hc, :].T)
        m["wv1"] = pack_rhs(Wv[hc, :].T)
        m["wo1"] = pack_lhsT(Wo.T[hc, :])
        m["wup1"] = shared["wup1"]
        m["wgate1"] = shared["wgate1"]
        m["wdn1"] = shared["wdn1"]
    return per_core


_L0_KEYS = ("x0T_full", "x0T_own", "su_ug_t", "wsc_ug_t", "su_dp_t",
            "wsc_dp_t", "bq0_t", "bk0_t", "bfc_t", "bv0_row",
            "wq0", "wk0", "wv0", "wo0", "wup0", "wgate0", "wdn0")
_L1_KEYS = ("wq1", "wk1", "wv1", "wo1", "wup1", "wgate1", "wdn1")

_MODE = "FUSED"


def kernel(**inputs):
    per_core = _stage_inputs(inputs)
    out = np.empty((NB, S, E), dtype=np.float32)
    if _MODE == "FUSED":
        nc = build_program("FUSED")
        in_maps = [{k: per_core[c][k] for k in _L0_KEYS + _L1_KEYS}
                   for c in range(8)]
        res = run_bass_kernel_spmd(nc, in_maps, core_ids=list(range(8)))
        for c in range(8):
            b, par = c >> 1, c & 1
            out[b, par * 2048:(par + 1) * 2048, :] = res.results[c]["outT"].T
    else:
        nc0 = build_program("L0")
        in_maps = [{k: per_core[c][k] for k in _L0_KEYS} for c in range(8)]
        res0 = run_bass_kernel_spmd(nc0, in_maps, core_ids=list(range(8)))
        nc1 = build_program("L1")
        in_maps = [dict(x1T=res0.results[c]["x1T"],
                        **{k: per_core[c][k] for k in _L1_KEYS})
                   for c in range(8)]
        res1 = run_bass_kernel_spmd(nc1, in_maps, core_ids=list(range(8)))
        for c in range(8):
            b, par = c >> 1, c & 1
            out[b, par * 2048:(par + 1) * 2048, :] = res1.results[c]["outT"].T
    return out


# version 05 (final)
# speedup vs baseline: 1.0879x; 1.0019x over previous
import numpy as np
import ml_dtypes
import concourse.bass as bass
import concourse.mybir as mybir
from concourse.bass_utils import run_bass_kernel_spmd

F32 = mybir.dt.float32
BF16 = mybir.dt.bfloat16
AF = mybir.ActivationFunctionType
ALU = mybir.AluOpType
BF16NP = ml_dtypes.bfloat16

E = 2048
S = 4096
NB = 4
NHL = 8
HD = 128
INTER = 8192
NFC = 16
NIC = 64
TT = 512
NT_FULL = 8
NT_OWN = 4
RS2048 = float(1.0 / np.sqrt(2048.0))
RS8192 = float(1.0 / np.sqrt(8192.0))
SMS = float(1.0 / np.sqrt(128.0))
EPS = 1e-5


def _had(n):
    i = np.arange(n)[:, None]
    j = np.arange(n)[None, :]
    return ((-1.0) ** np.bitwise_count(i & j)).astype(np.float32)


def pack_lhsT(w):
    K, M = w.shape
    return np.ascontiguousarray(
        w.reshape(K // 128, 128, M // 128, 128).transpose(2, 1, 0, 3))


def pack_rhs(w):
    K, M = w.shape
    return np.ascontiguousarray(w.reshape(K // 128, 128, M))


def _quads(nbits_total, sp):
    n = 1 << nbits_total
    b0, b1 = 1 << (2 * sp), 1 << (2 * sp + 1)
    out = []
    for c in range(n):
        if (c & b0) == 0 and (c & b1) == 0:
            out.append((c, c + b0, c + b1, c + b0 + b1))
    return out


class Prog:

    def __init__(self, nc, layers, T, C, SB, PS, SEM):
        self.nc = nc
        self.layers = layers
        self.T = T
        self.C = C
        self.SB = SB
        self.PS = PS
        self.SEM = SEM

    def start(self, role, eng):
        self.role = role
        self.eng = eng
        self.stoplim = getattr(self, "stoplim", None)
        self.skipset = getattr(self, "skipset", None)
        self.c = {k: 0 for k in ("qld", "qst", "qgp", "cc", "spe", "sact",
                                 "sdve", "qp0", "qp1", "qs0", "qs1",
                                 "qq0", "qq1", "qt0", "qt1", "spool",
                                 "bpe", "bact", "bdve", "bpool", "bsync",
                                 "phase")}
        self.lastwait = {}
        self.lu = {}

    def on(self, *roles):
        return self.role in roles

    def _wait(self, sem, val):
        if val <= 0:
            return
        key = sem
        if self.lastwait.get(key, -1) >= val:
            return
        self.lastwait[key] = val
        self.eng.wait_ge(self.SEM[sem], val)

    def w(self, role, *sems):
        if self.on(role):
            for s in sems:
                self._wait(s, self.c[s])

    def wv(self, role, sem, val):
        if self.on(role):
            self._wait(sem, val)

    def dma(self, q, out, in_, sem=None):
        qsem = sem or {"sync": "qld", "scalar": "qst", "pool": "qgp"}[q]
        qrole = {"sync": "sync", "scalar": "act", "pool": "pool"}[q]
        if self.on(qrole):
            self.eng.dma_start(out=out, in_=in_).then_inc(self.SEM[qsem], 16)
        self.c[qsem] += 16

    def mm(self, out, lhsT, rhs, start, stop, inc=False, skip=False):
        if self.on("pe"):
            i = self.eng.matmul(out, lhsT, rhs, start=start, stop=stop,
                                skip_group_check=skip)
            if inc:
                i.then_inc(self.SEM["spe"], 1)
        if inc:
            self.c["spe"] += 1

    def op(self, role, fn, inc=None):
        auto = {"dve": "sdve", "act": "sact"}.get(role)
        if self.on(role):
            if auto:
                self._wait(auto, self.c[auto])
            i = fn(self.eng)
            if auto:
                i.then_inc(self.SEM[auto], 1)
            if inc and inc != auto:
                i.then_inc(self.SEM[inc], 1)
        if auto:
            self.c[auto] += 1
        if inc and inc != auto:
            self.c[inc] += 1

    def rbank(self, key, sem):
        self.lu[key] = (sem, self.c[sem])

    def wbank(self, role, key):
        pu = self.lu.get(key)
        if pu:
            self.wv(role, *pu)

    def barrier(self):
        self.c["phase"] += 1
        ph = self.c["phase"]
        mine = {"pe": "bpe", "act": "bact", "dve": "bdve",
                "pool": "bpool", "sync": "bsync"}[self.role] \
            if self.role in ("pe", "act", "dve", "pool", "sync") else None
        for s in ("qld", "qst", "qgp", "qp0", "qp1", "qs0", "qs1",
                  "qq0", "qq1", "qt0", "qt1", "cc"):
            self._wait(s, self.c[s])
        if self.role in ("pe", "act", "dve", "pool"):
            self.eng.drain()
        self.eng.sem_inc(self.SEM[mine], 1)
        for b in ("bpe", "bact", "bdve", "bpool", "bsync"):
            self.c[b] = ph
            if b != mine:
                self._wait(b, ph)

    def chk(self):
        self.step += 1
        if self.skipset and self.step in self.skipset:
            return False
        return self.stoplim is None or self.step <= self.stoplim

    def program(self):
        self.step = 0
        self.stopped = False
        self.ph_setup()
        self.barrier()
        for L in self.layers:
            if L == 0:
                if self.chk():
                    self.ph_ln(0, "full")
                self.barrier()
                if self.chk():
                    self.ph_ln(0, "own")
            else:
                self.chk()
                if self.chk():
                    self.ph_ln(1, "own")
                    self.w("pool", "qst")
                    self.op("pool", lambda e: e.collective_compute(
                        "AllGather", ALU.bypass,
                        replica_groups=[[0, 1], [2, 3], [4, 5], [6, 7]],
                        ins=[self.T["xn1_own_cc"][:, :]],
                        outs=[self.T["xn1_full_cc"][:, :]]), inc="cc")
            self.barrier()
            if self.chk():
                self.ph_ffn1(L)
            self.barrier()
            if self.chk():
                self.ph_ffn2(L)
            self.barrier()
            if self.chk():
                self.ph_qkv(L)
            self.barrier()
            if self.chk():
                self.ph_bounds(L)
            self.barrier()
            if self.chk():
                self.ph_attn(L)
            self.barrier()
            if self.chk():
                self.ph_outproj(L)
            self.barrier()
            if self.chk():
                self.ph_resid(L)
            self.barrier()

    def ph_setup(self):
        T, C, SB = self.T, self.C, self.SB
        self.dma("sync", SB["h128"][:, :], C["h128"])
        self.dma("sync", SB["ones"][:, :], C["ones"])
        for o in range(4):
            self.dma("sync", SB["mask"][:, o, :], C["mask"][o])
        if 0 in self.layers:
            self.dma("sync", SB["su"][:, :], T["su_ug_t"])
            self.dma("sync", SB["mu"][:, 0:128], T["wsc_ug_t"])
            self.dma("sync", SB["mu"][:, 128:192], T["su_dp_t"])
            self.dma("sync", SB["mu"][:, 192:208], T["wsc_dp_t"])
            self.dma("sync", SB["bias"][:, 0:8], T["bq0_t"])
            self.dma("sync", SB["bias"][:, 8:9], T["bk0_t"])
            self.dma("sync", SB["bias"][:, 9:25], T["bfc_t"])
            self.dma("pool", SB["bvrow"][:, :], T["bv0_row"])
        self.w("dve", "qld", "qgp")
        A = SB["alpha"]
        W = SB["mu"]
        if 0 in self.layers:
            self.op("dve", lambda e: e.tensor_mul(A[:, 0:64], W[:, 0:64],
                                                  W[:, 128:192]))
            self.op("dve", lambda e: e.tensor_scalar_mul(A[:, 0:64], A[:, 0:64],
                                                         RS2048))
            self.op("dve", lambda e: e.tensor_scalar_mul(A[:, 64:128],
                                                         W[:, 64:128], RS2048))
            self.op("dve", lambda e: e.tensor_scalar_mul(A[:, 128:144],
                                                         W[:, 192:208], RS8192))
        self.op("dve", lambda e: e.memset(A[:, 144:145], 1.0))
        self.op("dve", lambda e: e.memset(self.SB["nt"][:, 6:7], EPS))
        self.op("dve", lambda e: e.memset(SB["bias"][:, 25:26], 0.0))

    def ph_ln(self, L, kind):
        T, SB, PS = self.T, self.SB, self.PS
        if L == 0 and kind == "full":
            ntiles, fwht, to_act = NT_FULL, False, False
            src = lambda t, fc: T["x0T_full"][fc * 128:(fc + 1) * 128,
                                              t * TT:(t + 1) * TT]
            dst = lambda t, fc: T["xnT_hbm"][fc * 128:(fc + 1) * 128,
                                             t * TT:(t + 1) * TT]
        elif L == 0:
            ntiles, fwht, to_act = NT_OWN, True, True
            src = lambda t, fc: T["x0T_own"][fc * 128:(fc + 1) * 128,
                                             t * TT:(t + 1) * TT]
            dst = None
        else:
            ntiles, fwht, to_act = NT_OWN, False, True
            src = lambda t, fc: T["x1T"][fc * 128:(fc + 1) * 128,
                                         t * TT:(t + 1) * TT]
            dst = lambda t, fc: T["xn1_own_cc"][fc * 128:(fc + 1) * 128,
                                                t * TT:(t + 1) * TT]
        AB = SB["actbuf"]
        for t in range(ntiles):
            self.w("sync", "sdve")
            for fc in range(NFC):
                self.dma("sync", SB["res32"][:, fc, :], src(t, fc))
            for fc in range(NFC):
                sl = fc % 2
                if fc == 0:
                    self.w("dve", "qld")
                self.w("dve", "spe")
                self.op("dve", lambda e, fc=fc, sl=sl: e.tensor_copy(
                    SB["xcb"][:, sl, :], SB["res32"][:, fc, :]))
                self.op("dve", lambda e, sl=sl: e.tensor_mul(
                    SB["sqb"][:, sl, :], SB["xcb"][:, sl, :],
                    SB["xcb"][:, sl, :]))
                self.w("pe", "sdve")
                if fc == 0:
                    self.w("pe", "sact")
                self.mm(PS[0][:, :], SB["ones"][:, :], SB["xcb"][:, sl, :],
                        start=(fc == 0), stop=(fc == NFC - 1))
                self.mm(PS[1][:, :], SB["ones"][:, :], SB["sqb"][:, sl, :],
                        start=(fc == 0), stop=(fc == NFC - 1),
                        inc=True)
            self.w("dve", "spe")
            self.op("dve", lambda e: e.tensor_scalar_mul(SB["mu"][:, :],
                                                         PS[0][:, :], 1.0 / E))
            self.op("dve", lambda e: e.tensor_scalar_mul(SB["var"][:, :],
                                                         PS[1][:, :], 1.0 / E))
            self.op("dve", lambda e: e.tensor_mul(SB["lnt"][:, :],
                                                  SB["mu"][:, :],
                                                  SB["mu"][:, :]))
            self.op("dve", lambda e: e.tensor_sub(SB["var"][:, :],
                                                  SB["var"][:, :],
                                                  SB["lnt"][:, :]))
            self.w("act", "sdve")
            self.op("act", lambda e: e.activation(SB["lnt"][:, :],
                                                  SB["var"][:, :], AF.Ln,
                                                  bias=SB["nt"][:, 6:7]))
            self.op("act", lambda e: e.activation(SB["rstd"][:, :],
                                                  SB["lnt"][:, :], AF.Exp,
                                                  scale=-0.5))
            self.w("dve", "sact", "qst")
            for fc in range(NFC):
                sl = fc % 2
                self.op("dve", lambda e, fc=fc, sl=sl: e.tensor_sub(
                    SB["xf"][:, 0, :], SB["res32"][:, fc, :], SB["mu"][:, :]))
                self.op("dve", lambda e, fc=fc, sl=sl: e.tensor_mul(
                    SB["xnb"][:, fc, :], SB["xf"][:, 0, :], SB["rstd"][:, :]))
                if to_act:
                    dstap = AB[:, fc * 2048 + t * TT: fc * 2048 + t * TT + TT]
                    if L == 0:
                        self.op("dve", lambda e, fc=fc, d=dstap: e.tensor_scalar_mul(
                            d, SB["xnb"][:, fc, :], SB["su"][:, fc:fc + 1]))
                    else:
                        self.op("dve", lambda e, fc=fc, d=dstap: e.tensor_copy(
                            d, SB["xnb"][:, fc, :]))
            self.op("dve", lambda e: e.engine_nop())
            if dst is not None:
                self.w("act", "sdve")
                for fc in range(NFC):
                    self.dma("scalar", dst(t, fc), SB["xnb"][:, fc, :])
            if fwht:
                A = lambda cc: AB[:, cc * 2048 + t * TT: cc * 2048 + t * TT + TT]
                BT = SB["btmp"]
                for sp in range(2):
                    for (c0, c1, c2, c3) in _quads(4, sp):
                        self.op("dve", lambda e, a=A(c0), b=A(c1): e.tensor_add(BT[:, 0, :], a, b))
                        self.op("dve", lambda e, a=A(c0), b=A(c1): e.tensor_sub(BT[:, 1, :], a, b))
                        self.op("dve", lambda e, a=A(c2), b=A(c3): e.tensor_add(BT[:, 2, :], a, b))
                        self.op("dve", lambda e, a=A(c2), b=A(c3): e.tensor_sub(BT[:, 3, :], a, b))
                        self.op("dve", lambda e, o=A(c0): e.tensor_add(o, BT[:, 0, :], BT[:, 2, :]))
                        self.op("dve", lambda e, o=A(c2): e.tensor_sub(o, BT[:, 0, :], BT[:, 2, :]))
                        self.op("dve", lambda e, o=A(c1): e.tensor_add(o, BT[:, 1, :], BT[:, 3, :]))
                        self.op("dve", lambda e, o=A(c3): e.tensor_sub(o, BT[:, 1, :], BT[:, 3, :]))
                self.op("dve", lambda e: e.engine_nop())
                for fc in range(NFC):
                    pb = 2 + (fc % 2)
                    self.w("pe", "sdve")
                    self.mm(self.PS[pb][:, :], SB["h128"][:, :], A(fc),
                            start=True, stop=True, inc=True)
                    self.w("dve", "spe")
                    self.op("dve", lambda e, fc=fc, pb=pb: e.tensor_copy(
                        A(fc), self.PS[pb][:, :]))

    def ph_ffn1(self, L):
        T, SB, PS = self.T, self.SB, self.PS
        AB = SB["actbuf"]
        wup = T["wup_pk"][L]
        wgt = T["wgate_pk"][L]
        for ob in range(NIC):
            sl = ob % 2
            pu = self.lu.get(("wblk", sl))
            if pu:
                self.wv("pool", *pu)
            qp = f"qp{sl}"
            self.dma("pool", SB["wblk"][:, sl, :, :], wup[ob], sem=qp)
            self.dma("pool", SB["wblk"][:, 2 + sl, :, :], wgt[ob], sem=qp)
            ldc = self.c[qp]
            for ts in range(4):
                bg = (ob * 4 + ts) % 2
                bu = 2 + (ob * 4 + ts) % 2
                if ts == 0:
                    self.wv("pe", qp, ldc)
                self.wbank("pe", ("gb", bg))
                for kc in range(NFC):
                    self.mm(PS[bg][:, :], SB["wblk"][:, 2 + sl, kc, :],
                            AB[:, kc * 2048 + ts * TT: kc * 2048 + ts * TT + TT],
                            start=(kc == 0), stop=(kc == NFC - 1),
                            inc=(kc == NFC - 1))
                self.wbank("act", ("sl", bg))
                self.w("act", "spe")
                gsc = SB["alpha"][:, 64 + ob:64 + ob + 1] if L == 0 else 1.0
                self.op("act", lambda e, bg=bg, gsc=gsc: e.activation(
                    SB["silu"][:, (0 if bg == 0 else 1), :], PS[bg][:, :],
                    AF.Silu, scale=gsc))
                self.rbank(("gb", bg), "sact")
                self.wbank("pe", ("ub", bu))
                for kc in range(NFC):
                    self.mm(PS[bu][:, :], SB["wblk"][:, sl, kc, :],
                            AB[:, kc * 2048 + ts * TT: kc * 2048 + ts * TT + TT],
                            start=(kc == 0), stop=(kc == NFC - 1),
                            inc=(kc == NFC - 1))
                ysl = (ob * 4 + ts) % 2
                qt = f"qt{ysl}"
                self.wbank("dve", ("yb", ysl))
                self.w("dve", "spe", "sact")
                usc = SB["alpha"][:, ob:ob + 1] if L == 0 \
                    else SB["alpha"][:, 144:145]
                self.op("dve", lambda e, bu=bu, bg=bg, ysl=ysl, usc=usc:
                        e.scalar_tensor_tensor(
                            SB["ybuf"][:, ysl, :], PS[bu][:, :], usc,
                            SB["silu"][:, (0 if bg == 0 else 1), :],
                            ALU.mult, ALU.mult))
                self.rbank(("ub", bu), "sdve")
                self.rbank(("sl", bg), "sdve")
                self.w("act", "sdve")
                self.dma("scalar", T["yH"][ob][:, ts * TT:(ts + 1) * TT],
                         SB["ybuf"][:, ysl, :], sem=qt)
                self.rbank(("yb", ysl), qt)
            self.lu[("wblk", sl)] = ("spe", self.c["spe"])

    def ph_ffn2(self, L):
        T, SB, PS = self.T, self.SB, self.PS
        AB = SB["actbuf"]
        wdn = T["wdn_pk"][L]
        res_src = (lambda t, fc: T["x0T_own"][fc * 128:(fc + 1) * 128,
                                              t * TT:(t + 1) * TT]) if L == 0 \
            else (lambda t, fc: T["x1T"][fc * 128:(fc + 1) * 128,
                                         t * TT:(t + 1) * TT])
        for t in range(NT_OWN):
            self.w("sync", "spe", "sdve")
            for ic in range(NIC):
                self.dma("sync", AB[:, ic * TT:(ic + 1) * TT],
                         T["yH"][ic][:, t * TT:(t + 1) * TT])
            for fc in range(NFC):
                self.dma("sync", SB["res32"][:, fc, :], res_src(t, fc))
            self.w("dve", "qld")
            A = lambda cc: AB[:, cc * TT:(cc + 1) * TT]
            if L == 0:
                ldc_pre = {}
                for mb in (0, 1):
                    sl = mb % 2
                    pu = self.lu.get(("wdnb", sl))
                    if pu:
                        self.wv("pool", *pu)
                    qpp = f"qp{sl}"
                    self.dma("pool", SB["buf32k"][:, sl * 8192:sl * 8192 + 8192],
                             wdn[mb].rearrange("p a b -> p (a b)"), sem=qpp)
                    ldc_pre[mb] = self.c[qpp]
                BT = SB["btmp"]
                PT = [SB["qg"][:, 0, :], SB["qg"][:, 1, :],
                      SB["pmask"][:, 0, :], SB["pmask"][:, 1, :]]
                self.wv("pool", "qld", self.c["qld"])
                for sp in range(3):
                    if sp > 0:
                        self.wv("dve", "spool", self.c["spool"])
                        self.wv("pool", "sdve", self.c["sdve"])
                    qs_ = _quads(6, sp)
                    for (c0, c1, c2, c3) in qs_[:12]:
                        self.op("dve", lambda e, a=A(c0), b=A(c1): e.tensor_add(BT[:, 0, :], a, b))
                        self.op("dve", lambda e, a=A(c0), b=A(c1): e.tensor_sub(BT[:, 1, :], a, b))
                        self.op("dve", lambda e, a=A(c2), b=A(c3): e.tensor_add(BT[:, 2, :], a, b))
                        self.op("dve", lambda e, a=A(c2), b=A(c3): e.tensor_sub(BT[:, 3, :], a, b))
                        self.op("dve", lambda e, o=A(c0): e.tensor_add(o, BT[:, 0, :], BT[:, 2, :]))
                        self.op("dve", lambda e, o=A(c2): e.tensor_sub(o, BT[:, 0, :], BT[:, 2, :]))
                        self.op("dve", lambda e, o=A(c1): e.tensor_add(o, BT[:, 1, :], BT[:, 3, :]))
                        self.op("dve", lambda e, o=A(c3): e.tensor_sub(o, BT[:, 1, :], BT[:, 3, :]))
                    for (c0, c1, c2, c3) in qs_[12:]:
                        for fn in (
                            lambda e, a=A(c0), b=A(c1): e.tensor_add(PT[0], a, b),
                            lambda e, a=A(c0), b=A(c1): e.tensor_sub(PT[1], a, b),
                            lambda e, a=A(c2), b=A(c3): e.tensor_add(PT[2], a, b),
                            lambda e, a=A(c2), b=A(c3): e.tensor_sub(PT[3], a, b),
                            lambda e, o=A(c0): e.tensor_add(o, PT[0], PT[2]),
                            lambda e, o=A(c2): e.tensor_sub(o, PT[0], PT[2]),
                            lambda e, o=A(c1): e.tensor_add(o, PT[1], PT[3]),
                            lambda e, o=A(c3): e.tensor_sub(o, PT[1], PT[3]),
                        ):
                            if self.on("pool"):
                                self._wait("spool", self.c["spool"])
                                fn(self.eng).then_inc(self.SEM["spool"], 1)
                            self.c["spool"] += 1
                self.op("dve", lambda e: e.engine_nop())
                self.wv("dve", "spool", self.c["spool"])
                self.op("dve", lambda e: e.engine_nop())
                bfly_done = self.c["sdve"]
                for ic in range(NIC):
                    pb = 2 + (ic % 2)
                    if ic == 0:
                        self.wv("pe", "sdve", bfly_done)
                    self.wbank("pe", ("hb", pb))
                    self.mm(PS[pb][:, :], SB["h128"][:, :], A(ic),
                            start=True, stop=True, inc=True)
                    self.w("dve", "spe")
                    self.op("dve", lambda e, ic=ic, pb=pb: e.tensor_copy(
                        A(ic), PS[pb][:, :]))
                    self.rbank(("hb", pb), "sdve")
                self.w("pe", "sdve")
            self.w("pe", "qld")
            for mb in range(NFC):
                sl = mb % 2
                off = sl * 8192
                qp = f"qp{sl}"
                if L == 0 and mb < 2:
                    ldc = ldc_pre[mb]
                else:
                    pu = self.lu.get(("wdnb", sl))
                    if pu:
                        self.wv("pool", *pu)
                    self.dma("pool", SB["buf32k"][:, off:off + 8192],
                             wdn[mb].rearrange("p a b -> p (a b)"), sem=qp)
                    ldc = self.c[qp]
                pb = 4 + (mb % 2)
                self.wv("pe", qp, ldc)
                self.wbank("pe", ("db", pb))
                for ic in range(NIC):
                    self.mm(PS[pb][:, :],
                            SB["buf32k"][:, off + ic * 128: off + (ic + 1) * 128],
                            A(ic), start=(ic == 0), stop=(ic == NIC - 1),
                            inc=(ic == NIC - 1))
                self.lu[("wdnb", sl)] = ("spe", self.c["spe"])
                dsc = SB["alpha"][:, 128 + mb:128 + mb + 1] if L == 0 \
                    else SB["alpha"][:, 144:145]
                osl = mb % 2
                qt = f"qt{osl}"
                self.wbank("dve", ("o32", osl))
                self.w("dve", "spe")
                self.op("dve", lambda e, pb=pb, mb=mb, osl=osl, dsc=dsc:
                        e.scalar_tensor_tensor(
                            SB["o32"][:, osl, :], PS[pb][:, :], dsc,
                            SB["res32"][:, mb, :], ALU.mult, ALU.add))
                self.rbank(("db", pb), "sdve")
                self.w("act", "sdve")
                self.dma("scalar",
                         T["x1pT"][mb * 128:(mb + 1) * 128, t * TT:(t + 1) * TT],
                         SB["o32"][:, osl, :], sem=qt)
                self.rbank(("o32", osl), qt)

    def ph_qkv(self, L):
        T, SB, PS = self.T, self.SB, self.PS
        AB = SB["actbuf"]
        DV = 128 if L == 0 else 1024
        nkh = 1 if L == 0 else NHL
        for kc in range(NFC):
            self.dma("pool", SB["buf32k"][:, kc * DV:(kc + 1) * DV],
                     T["wv_rhs"][L][kc])
        for th in range(2):
            self.w("sync", "spe")
            if L == 1 and th == 0:
                self.wv("sync", "cc", self.c["cc"])
            for fc in range(NFC):
                if L == 0:
                    srcap = T["xnT_hbm"][fc * 128:(fc + 1) * 128,
                                         th * 2048:(th + 1) * 2048]
                else:
                    srcap = T["xn1_full_cc"][th * 2048 + fc * 128:
                                             th * 2048 + (fc + 1) * 128, :]
                self.dma("sync", AB[:, fc * 2048:(fc + 1) * 2048], srcap)
            ldx = self.c["qld"]
            for (nmb, wpk, dstT, bias) in (
                (NHL, T["wq_pk"][L], T["qT_hbm"],
                 SB["bias"][:, 0:8] if L == 0 else None),
                (nkh, T["wk_pk"][L], T["kT_hbm"],
                 SB["bias"][:, 8:9] if L == 0 else None),
            ):
                for mb in range(nmb):
                    sl = mb % 2
                    pu = self.lu.get(("wblk", sl))
                    if pu:
                        self.wv("pool", *pu)
                    qp = f"qp{sl}"
                    self.dma("pool", SB["wblk"][:, sl, :, :], wpk[mb], sem=qp)
                    ldc = self.c[qp]
                    for tq in range(4):
                        pb = tq % 2
                        if tq == 0:
                            self.wv("pe", qp, ldc)
                            self.wv("pe", "qld", ldx)
                        self.wbank("pe", ("qb", pb))
                        for kc in range(NFC):
                            self.mm(PS[pb][:, :], SB["wblk"][:, sl, kc, :],
                                    AB[:, kc * 2048 + tq * TT:
                                       kc * 2048 + tq * TT + TT],
                                    start=(kc == 0), stop=(kc == NFC - 1),
                                    inc=(kc == NFC - 1))
                        esl = tq % 2
                        qt = f"qt{esl}"
                        self.wbank("dve", ("qev", esl))
                        self.w("dve", "spe")
                        if bias is not None:
                            bcol = bias[:, mb:mb + 1]
                            self.op("dve", lambda e, pb=pb, esl=esl, b=bcol:
                                    e.tensor_scalar_add(SB["qev"][:, esl, :],
                                                        PS[pb][:, :], b))
                        else:
                            self.op("dve", lambda e, pb=pb, esl=esl:
                                    e.tensor_copy(SB["qev"][:, esl, :],
                                                  PS[pb][:, :]))
                        self.rbank(("qb", pb), "sdve")
                        self.w("act", "sdve")
                        self.dma("scalar",
                                 dstT[mb * 128:(mb + 1) * 128,
                                      th * 2048 + tq * TT:
                                      th * 2048 + (tq + 1) * TT],
                                 SB["qev"][:, esl, :], sem=qt)
                        self.rbank(("qev", esl), qt)
                    self.lu[("wblk", sl)] = ("spe", self.c["spe"])
            for tsub in range(16):
                for dvg in range(DV // 512 if DV >= 512 else 1):
                    pb = 4 + (tsub % 2)
                    n = min(DV, 512)
                    if tsub == 0 and dvg == 0:
                        self.w("pe", "qgp")
                    self.wbank("pe", ("vb", pb))
                    if L == 0:
                        self.mm(PS[pb][:, 0:128], SB["ones"][0:1, :],
                                SB["bvrow"][:, :], start=True, stop=False,
                                skip=True)
                    for kc in range(NFC):
                        self.mm(PS[pb][:, 0:n],
                                AB[:, kc * 2048 + tsub * 128:
                                   kc * 2048 + tsub * 128 + 128],
                                SB["buf32k"][:, kc * DV + dvg * 512:
                                             kc * DV + dvg * 512 + n],
                                start=(kc == 0 and L == 1), stop=(kc == NFC - 1),
                                inc=(kc == NFC - 1), skip=(L == 0))
                    esl = tsub % 2
                    qt = f"qt{esl}"
                    self.wbank("dve", ("qev", esl))
                    self.w("dve", "spe")
                    self.op("dve", lambda e, pb=pb, esl=esl, n=n: e.tensor_copy(
                        SB["qev"][:, esl, 0:n], PS[pb][:, 0:n]))
                    self.rbank(("vb", pb), "sdve")
                    self.w("act", "sdve")
                    for hj in range(n // 128):
                        head = dvg * 4 + hj
                        tc = th * 16 + tsub
                        self.dma("scalar", T["v_hbm"][head, tc],
                                 SB["qev"][:, esl, hj * 128:(hj + 1) * 128],
                                 sem=qt)
                    self.rbank(("qev", esl), qt)

    def ph_bounds(self, L):
        T, SB, PS = self.T, self.SB, self.PS
        nkh = 1 if L == 0 else NHL
        for (nh, srcT, dst) in ((NHL, T["qT_hbm"], SB["mq2"]),
                                (nkh, T["kT_hbm"], SB["mk2"])):
            for h in range(nh):
                self.w("sync", "sdve")
                self.dma("sync", SB["buf32k"][:, 0:4096],
                         srcT[h * 128:(h + 1) * 128, :])
                self.w("dve", "qld")
                for g in range(8):
                    sl = g % 2
                    self.w("dve", "spe")
                    self.op("dve", lambda e, g=g, sl=sl: e.tensor_mul(
                        SB["sqb"][:, sl, :],
                        SB["buf32k"][:, g * TT:(g + 1) * TT],
                        SB["buf32k"][:, g * TT:(g + 1) * TT]))
                    pb = g % 2
                    self.w("pe", "sdve")
                    self.w("pe", "sact")
                    self.mm(PS[pb][:, :], SB["ones"][:, :], SB["sqb"][:, sl, :],
                            start=True, stop=True, inc=True)
                    self.w("dve", "spe")
                    self.op("dve", lambda e, pb=pb: e.reduce_max(
                        SB["nt"][:, 0:1], PS[pb][:, :], mybir.AxisListType.X))
                    if g == 0:
                        self.op("dve", lambda e, h=h, dst=dst: e.tensor_copy(
                            dst[:, h:h + 1], SB["nt"][:, 0:1]))
                    else:
                        self.op("dve", lambda e, h=h, dst=dst: e.tensor_max(
                            dst[:, h:h + 1], dst[:, h:h + 1], SB["nt"][:, 0:1]))
        self.w("act", "sdve")
        for h in range(NHL):
            kh = 0 if L == 0 else h
            self.op("act", lambda e, h=h: e.activation(
                SB["nt"][:, 1:2], SB["mq2"][:, h:h + 1], AF.Ln))
            self.op("act", lambda e, kh=kh: e.activation(
                SB["nt"][:, 2:3], SB["mk2"][:, kh:kh + 1], AF.Ln))
            self.w("dve", "sact")
            self.op("dve", lambda e: e.tensor_add(
                SB["nt"][:, 3:4], SB["nt"][:, 1:2], SB["nt"][:, 2:3]))
            self.w("act", "sdve")
            self.op("act", lambda e: e.activation(
                SB["nt"][:, 4:5], SB["nt"][:, 3:4], AF.Exp, scale=0.5))
            self.w("dve", "sact")
            self.op("dve", lambda e, h=h: e.tensor_scalar_mul(
                SB["bh"][:, h:h + 1], SB["nt"][:, 4:5], -SMS))

    def ph_attn(self, L):
        T, SB, PS = self.T, self.SB, self.PS
        for h in range(NHL):
            kv = 0 if L == 0 else h
            ksl = h % 2
            koff = ksl * 4096
            voff = 8192 + ksl * 4096
            qs = f"qs{ksl}"
            if not (L == 0 and h > 0):
                pu = self.lu.get(("kvs", ksl))
                if pu:
                    self.wv("sync", *pu)
                self.dma("sync", SB["buf32k"][:, koff:koff + 4096],
                         T["kT_hbm"][kv * 128:(kv + 1) * 128, :], sem=qs)
                for tc in range(32):
                    self.dma("sync", SB["buf32k"][:, voff + tc * 128:
                                                  voff + (tc + 1) * 128],
                             T["v_hbm"][kv, tc], sem=qs)
            if L == 0 and h > 0:
                koff = 0
                voff = 8192
                ksl = 0
                qs = "qs0"
            ldk = self.c[qs]
            for g in range(8):
                qsl = g % 2
                qq = f"qq{qsl}"
                self.w("sync", "spe")
                self.dma("sync", SB["qg"][:, qsl, :],
                         T["qT_hbm"][h * 128:(h + 1) * 128,
                                     g * TT:(g + 1) * TT], sem=qq)
                ldq = self.c[qq]
                nkc = 4 * g + 4
                bs_ = 2
                ba_ = 3 + 2 * (g % 2)
                for kc in range(nkc):
                    sb = (0, 1, 4)[kc % 3]
                    if kc == 0:
                        self.wv("pe", qq, ldq)
                        self.wv("pe", qs, ldk)
                        self.wbank("pe", ("sumb",))
                        self.wbank("pe", ("avb", g % 2))
                    self.wbank("pe", ("scb", sb))
                    self.mm(PS[sb][:, :],
                            SB["buf32k"][:, koff + kc * 128:koff + (kc + 1) * 128],
                            SB["qg"][:, qsl, :], start=True, stop=True,
                            inc=True)
                    psl = kc % 3
                    self.wbank("act", ("pslot", psl))
                    self.w("act", "spe")
                    self.op("act", lambda e, sb=sb, psl=psl, h=h: e.activation(
                        SB["pbuf"][:, psl, :], PS[sb][:, :], AF.Exp,
                        bias=SB["bh"][:, h:h + 1], scale=SMS))
                    self.rbank(("scb", sb), "sact")
                    if kc >= 4 * g:
                        o = kc - 4 * g
                        msl = kc % 2
                        self.wbank("dve", ("pm", msl))
                        self.w("dve", "sact")
                        self.op("dve", lambda e, psl=psl, msl=msl, o=o:
                                e.tensor_mul(SB["pmask"][:, msl, :],
                                             SB["pbuf"][:, psl, :],
                                             SB["mask"][:, o, :]))
                        self.rbank(("pslot", psl), "sdve")
                        used = SB["pmask"][:, msl, :]
                        self.w("pe", "sdve")
                    else:
                        used = SB["pbuf"][:, psl, :]
                        self.w("pe", "sact")
                    self.mm(PS[bs_][:, :], SB["ones"][:, :], used,
                            start=(kc == 0), stop=(kc == nkc - 1))
                    self.mm(PS[ba_][:, :],
                            SB["buf32k"][:, voff + kc * 128:voff + (kc + 1) * 128],
                            used, start=(kc == 0), stop=(kc == nkc - 1),
                            inc=True)
                    if kc >= 4 * g:
                        self.rbank(("pm", kc % 2), "spe")
                    else:
                        self.rbank(("pslot", psl), "spe")
                self.wbank("act", ("rsum",))
                self.w("act", "spe")
                self.op("act", lambda e, bs_=bs_: e.activation(
                    SB["lnt"][:, :], PS[bs_][:, :], AF.Ln))
                self.op("act", lambda e: e.activation(
                    SB["rsum"][:, 0, :], SB["lnt"][:, :], AF.Exp,
                    scale=-1.0))
                self.rbank(("sumb",), "sact")
                self.wbank("dve", ("aev", g % 2))
                self.w("dve", "sact")
                self.op("dve", lambda e, ba_=ba_, g=g: e.tensor_mul(
                    SB["attn_ev"][:, g % 2, :], PS[ba_][:, :],
                    SB["rsum"][:, 0, :]))
                self.rbank(("avb", g % 2), "sdve")
                self.rbank(("rsum",), "sdve")
                self.w("act", "sdve")
                qt = f"qt{g % 2}"
                self.dma("scalar",
                         T["attnT_hbm"][h * 128:(h + 1) * 128,
                                        g * TT:(g + 1) * TT],
                         SB["attn_ev"][:, g % 2, :], sem=qt)
                self.rbank(("aev", g % 2), qt)
            self.lu[("kvs", h % 2)] = ("spe", self.c["spe"])

    def ph_outproj(self, L):
        T, SB, PS = self.T, self.SB, self.PS
        AB = SB["actbuf"]
        wo = T["wo_pk"][L]
        for hc in range(NHL):
            self.dma("sync", AB[:, hc * 4096:(hc + 1) * 4096],
                     T["attnT_hbm"][hc * 128:(hc + 1) * 128, :])
        for mb in range(NFC):
            sl = mb % 2
            pu = self.lu.get(("wblk", sl))
            if pu:
                self.wv("pool", *pu)
            qp = f"qp{sl}"
            self.dma("pool", SB["wblk"][:, sl, 0:8, :], wo[mb], sem=qp)
            ldc = self.c[qp]
            for t8 in range(8):
                pb = t8 % 2
                if t8 == 0:
                    self.wv("pe", qp, ldc)
                    self.w("pe", "qld")
                self.wbank("pe", ("qb", pb))
                for kc in range(NHL):
                    self.mm(PS[pb][:, :], SB["wblk"][:, sl, kc, :],
                            AB[:, kc * 4096 + t8 * TT:
                               kc * 4096 + t8 * TT + TT],
                            start=(kc == 0), stop=(kc == NHL - 1),
                            inc=(kc == NHL - 1))
                esl = t8 % 2
                qt = f"qt{esl}"
                self.wbank("dve", ("qev", esl))
                self.w("dve", "spe")
                self.op("dve", lambda e, pb=pb, esl=esl: e.tensor_copy(
                    SB["qev"][:, esl, :], PS[pb][:, :]))
                self.rbank(("qb", pb), "sdve")
                self.w("act", "sdve")
                self.dma("scalar",
                         T["rs_in"][t8 // 4, mb * 128:(mb + 1) * 128,
                                    (t8 % 4) * TT:(t8 % 4 + 1) * TT],
                         SB["qev"][:, esl, :], sem=qt)
                self.rbank(("qev", esl), qt)
            self.lu[("wblk", sl)] = ("spe", self.c["spe"])
        self.w("pool", "qst", "qt0", "qt1")
        self.op("pool", lambda e: e.collective_compute(
            "ReduceScatter", ALU.add,
            replica_groups=[[0, 1], [2, 3], [4, 5], [6, 7]],
            ins=[T["rs_in"][:, :, :]], outs=[T["rs_out"][:, :]]), inc="cc")

    def ph_resid(self, L):
        T, SB = self.T, self.SB
        dstT = T["x1T"] if L == 0 else T["outT"]
        self.wv("sync", "cc", self.c["cc"])
        for t in range(NT_OWN):
            self.w("sync", "sdve")
            for fc in range(NFC):
                self.dma("sync", SB["xnb"][:, fc, :],
                         T["rs_out"][fc * 128:(fc + 1) * 128,
                                     t * TT:(t + 1) * TT])
                self.dma("sync", SB["res32"][:, fc, :],
                         T["x1pT"][fc * 128:(fc + 1) * 128,
                                   t * TT:(t + 1) * TT])
            self.w("dve", "qld")
            for fc in range(NFC):
                bcol = SB["bias"][:, 9 + fc:10 + fc] if L == 0 \
                    else SB["bias"][:, 25:26]
                osl = fc % 2
                qt = f"qt{osl}"
                self.wbank("dve", ("o32", osl))
                self.op("dve", lambda e, fc=fc, osl=osl, b=bcol:
                        e.scalar_tensor_tensor(
                            SB["o32"][:, osl, :], SB["xnb"][:, fc, :], b,
                            SB["res32"][:, fc, :], ALU.add, ALU.add))
                self.w("act", "sdve")
                self.dma("scalar",
                         dstT[fc * 128:(fc + 1) * 128, t * TT:(t + 1) * TT],
                         SB["o32"][:, osl, :], sem=qt)
                self.rbank(("o32", osl), qt)



def build_program(mode, taps=(), stoplim=None, skipset=None):
    layers = {"FUSED": [0, 1], "L0": [0], "L1": [1]}[mode]
    nc = bass.Bass("TRN2", target_bir_lowering=False, debug=False,
                   dynamic_dma_scratch_size=16384)
    T = {}

    def dram(name, shape, dtype, kind="Internal"):
        if name in taps and kind == "Internal":
            kind = "ExternalOutput"
        T[name] = nc.dram_tensor(name, shape, dtype, kind=kind).ap()
        return T[name]

    if 0 in layers:
        dram("x0T_full", [E, S], F32, "ExternalInput")
        dram("x0T_own", [E, 2048], F32, "ExternalInput")
        dram("su_ug_t", [128, 16], F32, "ExternalInput")
        dram("wsc_ug_t", [128, 128], F32, "ExternalInput")
        dram("su_dp_t", [128, 64], F32, "ExternalInput")
        dram("wsc_dp_t", [128, 16], F32, "ExternalInput")
        dram("bq0_t", [128, 8], F32, "ExternalInput")
        dram("bk0_t", [128, 1], F32, "ExternalInput")
        dram("bfc_t", [128, 16], F32, "ExternalInput")
        dram("bv0_row", [1, 128], F32, "ExternalInput")
    T["wq_pk"], T["wk_pk"], T["wv_rhs"], T["wo_pk"] = {}, {}, {}, {}
    T["wup_pk"], T["wgate_pk"], T["wdn_pk"] = {}, {}, {}
    for L in layers:
        T["wq_pk"][L] = dram(f"wq{L}", [8, 128, 16, 128], F32, "ExternalInput")
        T["wk_pk"][L] = dram(f"wk{L}", [1 if L == 0 else 8, 128, 16, 128],
                             F32, "ExternalInput")
        T["wv_rhs"][L] = dram(f"wv{L}", [16, 128, 128 if L == 0 else 1024],
                              F32, "ExternalInput")
        T["wo_pk"][L] = dram(f"wo{L}", [16, 128, 8, 128], F32, "ExternalInput")
        T["wup_pk"][L] = dram(f"wup{L}", [64, 128, 16, 128], F32,
                              "ExternalInput")
        T["wgate_pk"][L] = dram(f"wgate{L}", [64, 128, 16, 128], F32,
                                "ExternalInput")
        T["wdn_pk"][L] = dram(f"wdn{L}", [16, 128, 64, 128], F32,
                              "ExternalInput")
    if mode == "L0":
        dram("x1T", [E, 2048], F32, "ExternalOutput")
    elif mode == "L1":
        dram("x1T", [E, 2048], F32, "ExternalInput")
    else:
        dram("x1T", [E, 2048], F32, "Internal")
    if 1 in layers:
        dram("outT", [E, 2048], F32, "ExternalOutput")
    dram("xnT_hbm", [E, S], BF16)
    dram("qT_hbm", [1024, S], BF16)
    dram("kT_hbm", [1024, S], BF16)
    dram("v_hbm", [8, 32, 128, 128], BF16)
    dram("attnT_hbm", [1024, S], BF16)
    dram("x1pT", [E, 2048], F32)
    dram("yH", [64, 128, 2048], BF16)
    dram("rs_in", [2, E, 2048], BF16)
    dram("rs_out", [E, 2048], BF16)
    if "dbga" in taps:
        dram("dbga", [E, 2048], BF16)
    if 1 in layers:
        dram("xn1_own_cc", [E, 2048], BF16)
        dram("xn1_full_cc", [2 * E, 2048], BF16)

    C = {}
    C["h128"] = nc.inline_tensor(_had(128).astype(BF16NP), name="h128c").ap()
    C["ones"] = nc.inline_tensor(np.ones((128, 128), np.float32).astype(BF16NP),
                                 name="onesc").ap()
    mask = np.zeros((4, 128, 512), np.float32)
    for o in range(4):
        for p in range(128):
            mask[o, p, o * 128 + p:] = 1.0
    C["mask"] = nc.inline_tensor(mask.astype(BF16NP), name="maskc").ap()

    from contextlib import ExitStack
    _sb_specs = [
        ("actbuf", [128, 32768], BF16), ("buf32k", [128, 16384], BF16),
        ("wblk", [128, 4, 16, 128], BF16), ("res32", [128, 16, 512], F32),
        ("xnb", [128, 16, 512], BF16), ("xcb", [128, 2, 512], BF16),
        ("sqb", [128, 2, 512], BF16), ("xf", [128, 1, 512], F32),
        ("btmp", [128, 4, 512], BF16), ("mu", [128, 512], F32),
        ("var", [128, 512], F32), ("lnt", [128, 512], F32),
        ("rstd", [128, 512], F32), ("silu", [128, 2, 512], BF16),
        ("ybuf", [128, 2, 512], BF16), ("o32", [128, 2, 512], F32),
        ("qev", [128, 2, 512], BF16), ("qg", [128, 2, 512], BF16),
        ("pbuf", [128, 3, 512], BF16), ("pmask", [128, 2, 512], BF16),
        ("rsum", [128, 1, 512], F32), ("attn_ev", [128, 2, 512], BF16),
        ("mask", [128, 4, 512], BF16), ("h128", [128, 128], BF16),
        ("ones", [128, 128], BF16), ("su", [128, 16], F32),
        ("alpha", [128, 160], F32),
        ("bias", [128, 26], F32), ("bvrow", [1, 128], BF16),
        ("mq2", [128, 8], F32), ("mk2", [128, 8], F32),
        ("bh", [128, 8], F32), ("nt", [128, 8], F32),
    ]
    _sem_names = ("qld", "qst", "qgp", "cc", "spe", "sact", "sdve",
                  "qp0", "qp1", "qs0", "qs1", "qq0", "qq1", "qt0", "qt1",
                  "spool", "bpe", "bact", "bdve", "bpool", "bsync")
    with ExitStack() as _es:
        SB = {n: _es.enter_context(nc.sbuf_tensor(n, sh, dt))
              for (n, sh, dt) in _sb_specs}
        PS = [_es.enter_context(nc.psum_tensor(f"p{i}", [128, 512], F32))
              for i in range(6)]
        SEM = {n: _es.enter_context(nc.semaphore(n)) for n in _sem_names}
        block = _es.enter_context(nc.Block())
        prog = Prog(nc, layers, T, C, SB, PS, SEM)
        prog.stoplim = stoplim
        prog.skipset = skipset

        @block.tensor
        def _(eng):
            prog.start("pe", eng)
            prog.program()

        @block.scalar
        def _(eng):
            prog.start("act", eng)
            prog.program()

        @block.vector
        def _(eng):
            prog.start("dve", eng)
            prog.program()

        @block.gpsimd
        def _(eng):
            prog.start("pool", eng)
            prog.program()

        @block.sync
        def _(eng):
            prog.start("sync", eng)
            prog.program()

    return nc



def _stage_inputs(ins):
    per_core = [dict() for _ in range(8)]
    shared = {}

    w_qkv = np.asarray(ins["L0_attn_w_qkv"])
    b_qkv = np.asarray(ins["L0_attn_b_qkv"])
    w_fc = np.asarray(ins["L0_attn_w_fc"])
    b_fc = np.asarray(ins["L0_attn_b_fc"])
    shared["wk0"] = pack_lhsT(w_qkv[:, 2048:2176])
    shared["wv0"] = pack_rhs(w_qkv[:, 2176:2304])
    shared["bk0_t"] = np.ascontiguousarray(b_qkv[2048:2176].reshape(128, 1))
    shared["bv0_row"] = np.ascontiguousarray(b_qkv[2176:2304].reshape(1, 128))
    shared["bfc_t"] = np.ascontiguousarray(b_fc.reshape(16, 128).T)
    shared["su_ug_t"] = np.ascontiguousarray(
        np.asarray(ins["L0_ffn_SU_ug"]).reshape(16, 128).T)
    wsc = np.asarray(ins["L0_ffn_Wscale_ug"])
    shared["wsc_ug_t"] = np.ascontiguousarray(np.concatenate(
        [wsc[:8192].reshape(64, 128).T, wsc[8192:].reshape(64, 128).T], axis=1))
    shared["su_dp_t"] = np.ascontiguousarray(
        np.asarray(ins["L0_ffn_SU_dp"]).reshape(64, 128).T)
    shared["wsc_dp_t"] = np.ascontiguousarray(
        np.asarray(ins["L0_ffn_Wscale_dp"]).reshape(16, 128).T)
    shared["wup0"] = pack_lhsT(np.asarray(ins["L0_ffn_W_up"]).T)
    shared["wgate0"] = pack_lhsT(np.asarray(ins["L0_ffn_W_gate"]).T)
    shared["wdn0"] = pack_lhsT(np.asarray(ins["L0_ffn_W_down"]).T)
    Wq, Wk, Wv, Wo = (np.asarray(ins["L1_attn_Wq"]), np.asarray(ins["L1_attn_Wk"]),
                      np.asarray(ins["L1_attn_Wv"]), np.asarray(ins["L1_attn_Wo"]))
    wgu = np.asarray(ins["L1_ffn_w_gate_up"])
    shared["wgate1"] = pack_lhsT(wgu[:8192].T)
    shared["wup1"] = pack_lhsT(wgu[8192:].T)
    shared["wdn1"] = pack_lhsT(np.asarray(ins["L1_ffn_w_down"]).T)

    x0 = np.asarray(ins["L0_attn_x"])
    for c in range(8):
        b, par = c >> 1, c & 1
        m = per_core[c]
        x0T = np.ascontiguousarray(x0[b].T)
        m["x0T_full"] = x0T
        m["x0T_own"] = np.ascontiguousarray(x0T[:, par * 2048:(par + 1) * 2048])
        hc = slice(par * 1024, (par + 1) * 1024)
        m["wq0"] = pack_lhsT(w_qkv[:, hc])
        m["wk0"] = shared["wk0"]
        m["wv0"] = shared["wv0"]
        m["wo0"] = pack_lhsT(w_fc[hc, :])
        m["bq0_t"] = np.ascontiguousarray(b_qkv[hc].reshape(8, 128).T)
        for k in ("bk0_t", "bv0_row", "bfc_t", "su_ug_t", "wsc_ug_t",
                  "su_dp_t", "wsc_dp_t"):
            m[k] = shared[k]
        m["wup0"] = shared["wup0"]
        m["wgate0"] = shared["wgate0"]
        m["wdn0"] = shared["wdn0"]
        m["wq1"] = pack_lhsT(Wq[hc, :].T)
        m["wk1"] = pack_lhsT(Wk[hc, :].T)
        m["wv1"] = pack_rhs(Wv[hc, :].T)
        m["wo1"] = pack_lhsT(Wo.T[hc, :])
        m["wup1"] = shared["wup1"]
        m["wgate1"] = shared["wgate1"]
        m["wdn1"] = shared["wdn1"]
    return per_core


_L0_KEYS = ("x0T_full", "x0T_own", "su_ug_t", "wsc_ug_t", "su_dp_t",
            "wsc_dp_t", "bq0_t", "bk0_t", "bfc_t", "bv0_row",
            "wq0", "wk0", "wv0", "wo0", "wup0", "wgate0", "wdn0")
_L1_KEYS = ("wq1", "wk1", "wv1", "wo1", "wup1", "wgate1", "wdn1")

_MODE = "FUSED"


def kernel(**inputs):
    per_core = _stage_inputs(inputs)
    out = np.empty((NB, S, E), dtype=np.float32)
    if _MODE == "FUSED":
        nc = build_program("FUSED")
        in_maps = [{k: per_core[c][k] for k in _L0_KEYS + _L1_KEYS}
                   for c in range(8)]
        res = run_bass_kernel_spmd(nc, in_maps, core_ids=list(range(8)))
        for c in range(8):
            b, par = c >> 1, c & 1
            out[b, par * 2048:(par + 1) * 2048, :] = res.results[c]["outT"].T
    else:
        nc0 = build_program("L0")
        in_maps = [{k: per_core[c][k] for k in _L0_KEYS} for c in range(8)]
        res0 = run_bass_kernel_spmd(nc0, in_maps, core_ids=list(range(8)))
        nc1 = build_program("L1")
        in_maps = [dict(x1T=res0.results[c]["x1T"],
                        **{k: per_core[c][k] for k in _L1_KEYS})
                   for c in range(8)]
        res1 = run_bass_kernel_spmd(nc1, in_maps, core_ids=list(range(8)))
        for c in range(8):
            b, par = c >> 1, c & 1
            out[b, par * 2048:(par + 1) * 2048, :] = res1.results[c]["outT"].T
    return out


# version 06
# speedup vs baseline: 1.1393x; 1.0472x over previous
import numpy as np
import ml_dtypes
import concourse.bass as bass
import concourse.mybir as mybir
from concourse.bass_utils import run_bass_kernel_spmd

F32 = mybir.dt.float32
BF16 = mybir.dt.bfloat16
AF = mybir.ActivationFunctionType
ALU = mybir.AluOpType
BF16NP = ml_dtypes.bfloat16

E = 2048
S = 4096
NB = 4
NHL = 8
HD = 128
INTER = 8192
NFC = 16
NIC = 64
TT = 512
NT_FULL = 8
NT_OWN = 4
RS2048 = float(1.0 / np.sqrt(2048.0))
RS8192 = float(1.0 / np.sqrt(8192.0))
SMS = float(1.0 / np.sqrt(128.0))
EPS = 1e-5


def _had(n):
    i = np.arange(n)[:, None]
    j = np.arange(n)[None, :]
    return ((-1.0) ** np.bitwise_count(i & j)).astype(np.float32)


def pack_lhsT(w):
    K, M = w.shape
    return np.ascontiguousarray(
        w.reshape(K // 128, 128, M // 128, 128).transpose(2, 1, 0, 3))


def pack_rhs(w):
    K, M = w.shape
    return np.ascontiguousarray(w.reshape(K // 128, 128, M))


def _quads(nbits_total, sp):
    n = 1 << nbits_total
    b0, b1 = 1 << (2 * sp), 1 << (2 * sp + 1)
    out = []
    for c in range(n):
        if (c & b0) == 0 and (c & b1) == 0:
            out.append((c, c + b0, c + b1, c + b0 + b1))
    return out


class Prog:

    def __init__(self, nc, layers, T, C, SB, PS, SEM):
        self.nc = nc
        self.layers = layers
        self.T = T
        self.C = C
        self.SB = SB
        self.PS = PS
        self.SEM = SEM

    def start(self, role, eng):
        self.role = role
        self.eng = eng
        self.stoplim = getattr(self, "stoplim", None)
        self.skipset = getattr(self, "skipset", None)
        self.c = {k: 0 for k in ("qld", "qst", "qgp", "cc", "spe", "sact",
                                 "sdve", "qp0", "qp1", "qs0", "qs1",
                                 "qq0", "qq1", "qt0", "qt1", "spool",
                                 "bpe", "bact", "bdve", "bpool", "bsync",
                                 "phase")}
        self.lastwait = {}
        self.lu = {}

    def on(self, *roles):
        return self.role in roles

    def _wait(self, sem, val):
        if val <= 0:
            return
        key = sem
        if self.lastwait.get(key, -1) >= val:
            return
        self.lastwait[key] = val
        self.eng.wait_ge(self.SEM[sem], val)

    def w(self, role, *sems):
        if self.on(role):
            for s in sems:
                self._wait(s, self.c[s])

    def wv(self, role, sem, val):
        if self.on(role):
            self._wait(sem, val)

    def dma(self, q, out, in_, sem=None):
        qsem = sem or {"sync": "qld", "scalar": "qst", "pool": "qgp"}[q]
        qrole = {"sync": "sync", "scalar": "act", "pool": "pool"}[q]
        if self.on(qrole):
            self.eng.dma_start(out=out, in_=in_).then_inc(self.SEM[qsem], 16)
        self.c[qsem] += 16

    def mm(self, out, lhsT, rhs, start, stop, inc=False, skip=False):
        if self.on("pe"):
            i = self.eng.matmul(out, lhsT, rhs, start=start, stop=stop,
                                skip_group_check=skip)
            if inc:
                i.then_inc(self.SEM["spe"], 1)
        if inc:
            self.c["spe"] += 1

    def op(self, role, fn, inc=None):
        auto = {"dve": "sdve", "act": "sact"}.get(role)
        if self.on(role):
            i = fn(self.eng)
            if auto:
                i.then_inc(self.SEM[auto], 1)
            if inc and inc != auto:
                i.then_inc(self.SEM[inc], 1)
        if auto:
            self.c[auto] += 1
        if inc and inc != auto:
            self.c[inc] += 1

    def rbank(self, key, sem):
        self.lu[key] = (sem, self.c[sem])

    def wbank(self, role, key):
        pu = self.lu.get(key)
        if pu:
            self.wv(role, *pu)

    def barrier(self):
        self.c["phase"] += 1
        ph = self.c["phase"]
        mine = {"pe": "bpe", "act": "bact", "dve": "bdve",
                "pool": "bpool", "sync": "bsync"}[self.role] \
            if self.role in ("pe", "act", "dve", "pool", "sync") else None
        for s in ("qld", "qst", "qgp", "qp0", "qp1", "qs0", "qs1",
                  "qq0", "qq1", "qt0", "qt1", "cc"):
            self._wait(s, self.c[s])
        if self.role in ("pe", "act", "dve", "pool"):
            self.eng.drain()
        self.eng.sem_inc(self.SEM[mine], 1)
        for b in ("bpe", "bact", "bdve", "bpool", "bsync"):
            self.c[b] = ph
            if b != mine:
                self._wait(b, ph)

    def chk(self):
        self.step += 1
        if self.skipset and self.step in self.skipset:
            return False
        return self.stoplim is None or self.step <= self.stoplim

    def program(self):
        self.step = 0
        self.stopped = False
        self.ph_setup()
        self.barrier()
        for L in self.layers:
            if L == 0:
                if self.chk():
                    self.ph_ln(0, "full")
                self.barrier()
                if self.chk():
                    self.ph_ln(0, "own")
            else:
                self.chk()
                if self.chk():
                    self.ph_ln(1, "own")
                    self.w("pool", "qst")
                    self.op("pool", lambda e: e.collective_compute(
                        "AllGather", ALU.bypass,
                        replica_groups=[[0, 1], [2, 3], [4, 5], [6, 7]],
                        ins=[self.T["xn1_own_cc"][:, :]],
                        outs=[self.T["xn1_full_cc"][:, :]]), inc="cc")
            self.barrier()
            if self.chk():
                self.ph_ffn1(L)
            self.barrier()
            if self.chk():
                self.ph_ffn2(L)
            self.barrier()
            if self.chk():
                self.ph_qkv(L)
            self.barrier()
            if self.chk():
                self.ph_bounds(L)
            self.barrier()
            if self.chk():
                self.ph_attn(L)
            self.barrier()
            if self.chk():
                self.ph_outproj(L)
            self.barrier()
            if self.chk():
                self.ph_resid(L)
            self.barrier()

    def ph_setup(self):
        T, C, SB = self.T, self.C, self.SB
        self.dma("sync", SB["h128"][:, :], C["h128"])
        self.dma("sync", SB["ones"][:, :], C["ones"])
        for o in range(4):
            self.dma("sync", SB["mask"][:, o, :], C["mask"][o])
        if 0 in self.layers:
            self.dma("sync", SB["su"][:, :], T["su_ug_t"])
            self.dma("sync", SB["mu"][:, 0:128], T["wsc_ug_t"])
            self.dma("sync", SB["mu"][:, 128:192], T["su_dp_t"])
            self.dma("sync", SB["mu"][:, 192:208], T["wsc_dp_t"])
            self.dma("sync", SB["bias"][:, 0:8], T["bq0_t"])
            self.dma("sync", SB["bias"][:, 8:9], T["bk0_t"])
            self.dma("sync", SB["bias"][:, 9:25], T["bfc_t"])
            self.dma("pool", SB["bvrow"][:, :], T["bv0_row"])
        self.w("dve", "qld", "qgp")
        A = SB["alpha"]
        W = SB["mu"]
        if 0 in self.layers:
            self.op("dve", lambda e: e.tensor_mul(A[:, 0:64], W[:, 0:64],
                                                  W[:, 128:192]))
            self.op("dve", lambda e: e.tensor_scalar_mul(A[:, 0:64], A[:, 0:64],
                                                         RS2048))
            self.op("dve", lambda e: e.tensor_scalar_mul(A[:, 64:128],
                                                         W[:, 64:128], RS2048))
            self.op("dve", lambda e: e.tensor_scalar_mul(A[:, 128:144],
                                                         W[:, 192:208], RS8192))
        self.op("dve", lambda e: e.memset(A[:, 144:145], 1.0))
        self.op("dve", lambda e: e.memset(self.SB["nt"][:, 6:7], EPS))
        self.op("dve", lambda e: e.memset(SB["bias"][:, 25:26], 0.0))

    def ph_ln(self, L, kind):
        T, SB, PS = self.T, self.SB, self.PS
        if L == 0 and kind == "full":
            ntiles, fwht, to_act = NT_FULL, False, False
            src = lambda t, fc: T["x0T_full"][fc * 128:(fc + 1) * 128,
                                              t * TT:(t + 1) * TT]
            dst = lambda t, fc: T["xnT_hbm"][fc * 128:(fc + 1) * 128,
                                             t * TT:(t + 1) * TT]
        elif L == 0:
            ntiles, fwht, to_act = NT_OWN, True, True
            src = lambda t, fc: T["x0T_own"][fc * 128:(fc + 1) * 128,
                                             t * TT:(t + 1) * TT]
            dst = None
        else:
            ntiles, fwht, to_act = NT_OWN, False, True
            src = lambda t, fc: T["x1T"][fc * 128:(fc + 1) * 128,
                                         t * TT:(t + 1) * TT]
            dst = lambda t, fc: T["xn1_own_cc"][fc * 128:(fc + 1) * 128,
                                                t * TT:(t + 1) * TT]
        AB = SB["actbuf"]
        for t in range(ntiles):
            self.w("sync", "sdve")
            for fc in range(NFC):
                self.dma("sync", SB["res32"][:, fc, :], src(t, fc))
            for fc in range(NFC):
                sl = fc % 2
                if fc == 0:
                    self.w("dve", "qld")
                self.w("dve", "spe")
                self.op("dve", lambda e, fc=fc, sl=sl: e.tensor_copy(
                    SB["xcb"][:, sl, :], SB["res32"][:, fc, :]))
                self.op("dve", lambda e, sl=sl: e.tensor_mul(
                    SB["sqb"][:, sl, :], SB["xcb"][:, sl, :],
                    SB["xcb"][:, sl, :]))
                self.w("pe", "sdve")
                if fc == 0:
                    self.w("pe", "sact")
                self.mm(PS[0][:, :], SB["ones"][:, :], SB["xcb"][:, sl, :],
                        start=(fc == 0), stop=(fc == NFC - 1))
                self.mm(PS[1][:, :], SB["ones"][:, :], SB["sqb"][:, sl, :],
                        start=(fc == 0), stop=(fc == NFC - 1),
                        inc=True)
            self.w("dve", "spe")
            self.op("dve", lambda e: e.tensor_scalar_mul(SB["mu"][:, :],
                                                         PS[0][:, :], 1.0 / E))
            self.op("dve", lambda e: e.tensor_scalar_mul(SB["var"][:, :],
                                                         PS[1][:, :], 1.0 / E))
            self.op("dve", lambda e: e.tensor_mul(SB["lnt"][:, :],
                                                  SB["mu"][:, :],
                                                  SB["mu"][:, :]))
            self.op("dve", lambda e: e.tensor_sub(SB["var"][:, :],
                                                  SB["var"][:, :],
                                                  SB["lnt"][:, :]))
            self.w("act", "sdve")
            self.op("act", lambda e: e.activation(SB["lnt"][:, :],
                                                  SB["var"][:, :], AF.Ln,
                                                  bias=SB["nt"][:, 6:7]))
            self.op("act", lambda e: e.activation(SB["rstd"][:, :],
                                                  SB["lnt"][:, :], AF.Exp,
                                                  scale=-0.5))
            self.w("dve", "sact", "qst")
            for fc in range(NFC):
                sl = fc % 2
                self.op("dve", lambda e, fc=fc, sl=sl: e.tensor_sub(
                    SB["xf"][:, 0, :], SB["res32"][:, fc, :], SB["mu"][:, :]))
                self.op("dve", lambda e, fc=fc, sl=sl: e.tensor_mul(
                    SB["xnb"][:, fc, :], SB["xf"][:, 0, :], SB["rstd"][:, :]))
                if to_act:
                    dstap = AB[:, fc * 2048 + t * TT: fc * 2048 + t * TT + TT]
                    if L == 0:
                        self.op("dve", lambda e, fc=fc, d=dstap: e.tensor_scalar_mul(
                            d, SB["xnb"][:, fc, :], SB["su"][:, fc:fc + 1]))
                    else:
                        self.op("dve", lambda e, fc=fc, d=dstap: e.tensor_copy(
                            d, SB["xnb"][:, fc, :]))
            self.op("dve", lambda e: e.engine_nop())
            if dst is not None:
                self.w("act", "sdve")
                for fc in range(NFC):
                    self.dma("scalar", dst(t, fc), SB["xnb"][:, fc, :])
            if fwht:
                A = lambda cc: AB[:, cc * 2048 + t * TT: cc * 2048 + t * TT + TT]
                BT = SB["btmp"]
                for sp in range(2):
                    for (c0, c1, c2, c3) in _quads(4, sp):
                        self.op("dve", lambda e, a=A(c0), b=A(c1): e.tensor_add(BT[:, 0, :], a, b))
                        self.op("dve", lambda e, a=A(c0), b=A(c1): e.tensor_sub(BT[:, 1, :], a, b))
                        self.op("dve", lambda e, a=A(c2), b=A(c3): e.tensor_add(BT[:, 2, :], a, b))
                        self.op("dve", lambda e, a=A(c2), b=A(c3): e.tensor_sub(BT[:, 3, :], a, b))
                        self.op("dve", lambda e, o=A(c0): e.tensor_add(o, BT[:, 0, :], BT[:, 2, :]))
                        self.op("dve", lambda e, o=A(c2): e.tensor_sub(o, BT[:, 0, :], BT[:, 2, :]))
                        self.op("dve", lambda e, o=A(c1): e.tensor_add(o, BT[:, 1, :], BT[:, 3, :]))
                        self.op("dve", lambda e, o=A(c3): e.tensor_sub(o, BT[:, 1, :], BT[:, 3, :]))
                self.op("dve", lambda e: e.engine_nop())
                for fc in range(NFC):
                    pb = 2 + (fc % 2)
                    self.w("pe", "sdve")
                    self.mm(self.PS[pb][:, :], SB["h128"][:, :], A(fc),
                            start=True, stop=True, inc=True)
                    self.w("dve", "spe")
                    self.op("dve", lambda e, fc=fc, pb=pb: e.tensor_copy(
                        A(fc), self.PS[pb][:, :]))

    def ph_ffn1(self, L):
        T, SB, PS = self.T, self.SB, self.PS
        AB = SB["actbuf"]
        wup = T["wup_pk"][L]
        wgt = T["wgate_pk"][L]
        for ob in range(NIC):
            sl = ob % 2
            pu = self.lu.get(("wblk", sl))
            if pu:
                self.wv("pool", *pu)
            qp = f"qp{sl}"
            self.dma("pool", SB["wblk"][:, sl, :, :], wup[ob], sem=qp)
            self.dma("pool", SB["wblk"][:, 2 + sl, :, :], wgt[ob], sem=qp)
            ldc = self.c[qp]
            for ts in range(4):
                bg = (ob * 4 + ts) % 2
                bu = 2 + (ob * 4 + ts) % 2
                if ts == 0:
                    self.wv("pe", qp, ldc)
                self.wbank("pe", ("gb", bg))
                for kc in range(NFC):
                    self.mm(PS[bg][:, :], SB["wblk"][:, 2 + sl, kc, :],
                            AB[:, kc * 2048 + ts * TT: kc * 2048 + ts * TT + TT],
                            start=(kc == 0), stop=(kc == NFC - 1),
                            inc=(kc == NFC - 1))
                self.wbank("act", ("sl", bg))
                self.w("act", "spe")
                gsc = SB["alpha"][:, 64 + ob:64 + ob + 1] if L == 0 else 1.0
                self.op("act", lambda e, bg=bg, gsc=gsc: e.activation(
                    SB["silu"][:, (0 if bg == 0 else 1), :], PS[bg][:, :],
                    AF.Silu, scale=gsc))
                self.rbank(("gb", bg), "sact")
                self.wbank("pe", ("ub", bu))
                for kc in range(NFC):
                    self.mm(PS[bu][:, :], SB["wblk"][:, sl, kc, :],
                            AB[:, kc * 2048 + ts * TT: kc * 2048 + ts * TT + TT],
                            start=(kc == 0), stop=(kc == NFC - 1),
                            inc=(kc == NFC - 1))
                ysl = (ob * 4 + ts) % 2
                qt = f"qt{ysl}"
                self.wbank("dve", ("yb", ysl))
                self.w("dve", "spe", "sact")
                usc = SB["alpha"][:, ob:ob + 1] if L == 0 \
                    else SB["alpha"][:, 144:145]
                self.op("dve", lambda e, bu=bu, bg=bg, ysl=ysl, usc=usc:
                        e.scalar_tensor_tensor(
                            SB["ybuf"][:, ysl, :], PS[bu][:, :], usc,
                            SB["silu"][:, (0 if bg == 0 else 1), :],
                            ALU.mult, ALU.mult))
                self.rbank(("ub", bu), "sdve")
                self.rbank(("sl", bg), "sdve")
                self.w("act", "sdve")
                self.dma("scalar", T["yH"][ob][:, ts * TT:(ts + 1) * TT],
                         SB["ybuf"][:, ysl, :], sem=qt)
                self.rbank(("yb", ysl), qt)
            self.lu[("wblk", sl)] = ("spe", self.c["spe"])

    def ph_ffn2(self, L):
        T, SB, PS = self.T, self.SB, self.PS
        AB = SB["actbuf"]
        wdn = T["wdn_pk"][L]
        res_src = (lambda t, fc: T["x0T_own"][fc * 128:(fc + 1) * 128,
                                              t * TT:(t + 1) * TT]) if L == 0 \
            else (lambda t, fc: T["x1T"][fc * 128:(fc + 1) * 128,
                                         t * TT:(t + 1) * TT])
        for t in range(NT_OWN):
            self.w("sync", "spe", "sdve")
            for ic in range(NIC):
                self.dma("sync", AB[:, ic * TT:(ic + 1) * TT],
                         T["yH"][ic][:, t * TT:(t + 1) * TT])
            for fc in range(NFC):
                self.dma("sync", SB["res32"][:, fc, :], res_src(t, fc))
            self.w("dve", "qld")
            A = lambda cc: AB[:, cc * TT:(cc + 1) * TT]
            if L == 0:
                ldc_pre = {}
                for mb in (0, 1):
                    sl = mb % 2
                    pu = self.lu.get(("wdnb", sl))
                    if pu:
                        self.wv("pool", *pu)
                    qpp = f"qp{sl}"
                    self.dma("pool", SB["buf32k"][:, sl * 8192:sl * 8192 + 8192],
                             wdn[mb].rearrange("p a b -> p (a b)"), sem=qpp)
                    ldc_pre[mb] = self.c[qpp]
                BT = SB["btmp"]
                PT = [SB["qg"][:, 0, :], SB["qg"][:, 1, :],
                      SB["pmask"][:, 0, :], SB["pmask"][:, 1, :]]
                self.wv("pool", "qld", self.c["qld"])
                for sp in range(3):
                    if sp > 0:
                        self.wv("dve", "spool", self.c["spool"])
                        self.wv("pool", "sdve", self.c["sdve"])
                    qs_ = _quads(6, sp)
                    for (c0, c1, c2, c3) in qs_[:12]:
                        self.op("dve", lambda e, a=A(c0), b=A(c1): e.tensor_add(BT[:, 0, :], a, b))
                        self.op("dve", lambda e, a=A(c0), b=A(c1): e.tensor_sub(BT[:, 1, :], a, b))
                        self.op("dve", lambda e, a=A(c2), b=A(c3): e.tensor_add(BT[:, 2, :], a, b))
                        self.op("dve", lambda e, a=A(c2), b=A(c3): e.tensor_sub(BT[:, 3, :], a, b))
                        self.op("dve", lambda e, o=A(c0): e.tensor_add(o, BT[:, 0, :], BT[:, 2, :]))
                        self.op("dve", lambda e, o=A(c2): e.tensor_sub(o, BT[:, 0, :], BT[:, 2, :]))
                        self.op("dve", lambda e, o=A(c1): e.tensor_add(o, BT[:, 1, :], BT[:, 3, :]))
                        self.op("dve", lambda e, o=A(c3): e.tensor_sub(o, BT[:, 1, :], BT[:, 3, :]))
                    for (c0, c1, c2, c3) in qs_[12:]:
                        for fn in (
                            lambda e, a=A(c0), b=A(c1): e.tensor_add(PT[0], a, b),
                            lambda e, a=A(c0), b=A(c1): e.tensor_sub(PT[1], a, b),
                            lambda e, a=A(c2), b=A(c3): e.tensor_add(PT[2], a, b),
                            lambda e, a=A(c2), b=A(c3): e.tensor_sub(PT[3], a, b),
                            lambda e, o=A(c0): e.tensor_add(o, PT[0], PT[2]),
                            lambda e, o=A(c2): e.tensor_sub(o, PT[0], PT[2]),
                            lambda e, o=A(c1): e.tensor_add(o, PT[1], PT[3]),
                            lambda e, o=A(c3): e.tensor_sub(o, PT[1], PT[3]),
                        ):
                            if self.on("pool"):
                                fn(self.eng).then_inc(self.SEM["spool"], 1)
                            self.c["spool"] += 1
                self.op("dve", lambda e: e.engine_nop())
                self.wv("dve", "spool", self.c["spool"])
                self.op("dve", lambda e: e.engine_nop())
                bfly_done = self.c["sdve"]
                for ic in range(NIC):
                    pb = 2 + (ic % 2)
                    if ic == 0:
                        self.wv("pe", "sdve", bfly_done)
                    self.wbank("pe", ("hb", pb))
                    self.mm(PS[pb][:, :], SB["h128"][:, :], A(ic),
                            start=True, stop=True, inc=True)
                    self.w("dve", "spe")
                    self.op("dve", lambda e, ic=ic, pb=pb: e.tensor_copy(
                        A(ic), PS[pb][:, :]))
                    self.rbank(("hb", pb), "sdve")
                self.w("pe", "sdve")
            self.w("pe", "qld")
            for mb in range(NFC):
                sl = mb % 2
                off = sl * 8192
                qp = f"qp{sl}"
                if L == 0 and mb < 2:
                    ldc = ldc_pre[mb]
                else:
                    pu = self.lu.get(("wdnb", sl))
                    if pu:
                        self.wv("pool", *pu)
                    self.dma("pool", SB["buf32k"][:, off:off + 8192],
                             wdn[mb].rearrange("p a b -> p (a b)"), sem=qp)
                    ldc = self.c[qp]
                pb = 4 + (mb % 2)
                self.wv("pe", qp, ldc)
                self.wbank("pe", ("db", pb))
                for ic in range(NIC):
                    self.mm(PS[pb][:, :],
                            SB["buf32k"][:, off + ic * 128: off + (ic + 1) * 128],
                            A(ic), start=(ic == 0), stop=(ic == NIC - 1),
                            inc=(ic == NIC - 1))
                self.lu[("wdnb", sl)] = ("spe", self.c["spe"])
                dsc = SB["alpha"][:, 128 + mb:128 + mb + 1] if L == 0 \
                    else SB["alpha"][:, 144:145]
                osl = mb % 2
                qt = f"qt{osl}"
                self.wbank("dve", ("o32", osl))
                self.w("dve", "spe")
                self.op("dve", lambda e, pb=pb, mb=mb, osl=osl, dsc=dsc:
                        e.scalar_tensor_tensor(
                            SB["o32"][:, osl, :], PS[pb][:, :], dsc,
                            SB["res32"][:, mb, :], ALU.mult, ALU.add))
                self.rbank(("db", pb), "sdve")
                self.w("act", "sdve")
                self.dma("scalar",
                         T["x1pT"][mb * 128:(mb + 1) * 128, t * TT:(t + 1) * TT],
                         SB["o32"][:, osl, :], sem=qt)
                self.rbank(("o32", osl), qt)

    def ph_qkv(self, L):
        T, SB, PS = self.T, self.SB, self.PS
        AB = SB["actbuf"]
        DV = 128 if L == 0 else 1024
        nkh = 1 if L == 0 else NHL
        for kc in range(NFC):
            self.dma("pool", SB["buf32k"][:, kc * DV:(kc + 1) * DV],
                     T["wv_rhs"][L][kc])
        for th in range(2):
            self.w("sync", "spe")
            if L == 1 and th == 0:
                self.wv("sync", "cc", self.c["cc"])
            for fc in range(NFC):
                if L == 0:
                    srcap = T["xnT_hbm"][fc * 128:(fc + 1) * 128,
                                         th * 2048:(th + 1) * 2048]
                else:
                    srcap = T["xn1_full_cc"][th * 2048 + fc * 128:
                                             th * 2048 + (fc + 1) * 128, :]
                self.dma("sync", AB[:, fc * 2048:(fc + 1) * 2048], srcap)
            ldx = self.c["qld"]
            for (nmb, wpk, dstT, bias) in (
                (NHL, T["wq_pk"][L], T["qT_hbm"],
                 SB["bias"][:, 0:8] if L == 0 else None),
                (nkh, T["wk_pk"][L], T["kT_hbm"],
                 SB["bias"][:, 8:9] if L == 0 else None),
            ):
                for mb in range(nmb):
                    sl = mb % 2
                    pu = self.lu.get(("wblk", sl))
                    if pu:
                        self.wv("pool", *pu)
                    qp = f"qp{sl}"
                    self.dma("pool", SB["wblk"][:, sl, :, :], wpk[mb], sem=qp)
                    ldc = self.c[qp]
                    for tq in range(4):
                        pb = tq % 2
                        if tq == 0:
                            self.wv("pe", qp, ldc)
                            self.wv("pe", "qld", ldx)
                        self.wbank("pe", ("qb", pb))
                        for kc in range(NFC):
                            self.mm(PS[pb][:, :], SB["wblk"][:, sl, kc, :],
                                    AB[:, kc * 2048 + tq * TT:
                                       kc * 2048 + tq * TT + TT],
                                    start=(kc == 0), stop=(kc == NFC - 1),
                                    inc=(kc == NFC - 1))
                        esl = tq % 2
                        qt = f"qt{esl}"
                        self.wbank("dve", ("qev", esl))
                        self.w("dve", "spe")
                        if bias is not None:
                            bcol = bias[:, mb:mb + 1]
                            self.op("dve", lambda e, pb=pb, esl=esl, b=bcol:
                                    e.tensor_scalar_add(SB["qev"][:, esl, :],
                                                        PS[pb][:, :], b))
                        else:
                            self.op("dve", lambda e, pb=pb, esl=esl:
                                    e.tensor_copy(SB["qev"][:, esl, :],
                                                  PS[pb][:, :]))
                        self.rbank(("qb", pb), "sdve")
                        self.w("act", "sdve")
                        self.dma("scalar",
                                 dstT[mb * 128:(mb + 1) * 128,
                                      th * 2048 + tq * TT:
                                      th * 2048 + (tq + 1) * TT],
                                 SB["qev"][:, esl, :], sem=qt)
                        self.rbank(("qev", esl), qt)
                    self.lu[("wblk", sl)] = ("spe", self.c["spe"])
            for tsub in range(16):
                for dvg in range(DV // 512 if DV >= 512 else 1):
                    pb = 4 + (tsub % 2)
                    n = min(DV, 512)
                    if tsub == 0 and dvg == 0:
                        self.w("pe", "qgp")
                    self.wbank("pe", ("vb", pb))
                    if L == 0:
                        self.mm(PS[pb][:, 0:128], SB["ones"][0:1, :],
                                SB["bvrow"][:, :], start=True, stop=False,
                                skip=True)
                    for kc in range(NFC):
                        self.mm(PS[pb][:, 0:n],
                                AB[:, kc * 2048 + tsub * 128:
                                   kc * 2048 + tsub * 128 + 128],
                                SB["buf32k"][:, kc * DV + dvg * 512:
                                             kc * DV + dvg * 512 + n],
                                start=(kc == 0 and L == 1), stop=(kc == NFC - 1),
                                inc=(kc == NFC - 1), skip=(L == 0))
                    esl = tsub % 2
                    qt = f"qt{esl}"
                    self.wbank("dve", ("qev", esl))
                    self.w("dve", "spe")
                    self.op("dve", lambda e, pb=pb, esl=esl, n=n: e.tensor_copy(
                        SB["qev"][:, esl, 0:n], PS[pb][:, 0:n]))
                    self.rbank(("vb", pb), "sdve")
                    self.w("act", "sdve")
                    for hj in range(n // 128):
                        head = dvg * 4 + hj
                        tc = th * 16 + tsub
                        self.dma("scalar", T["v_hbm"][head, tc],
                                 SB["qev"][:, esl, hj * 128:(hj + 1) * 128],
                                 sem=qt)
                    self.rbank(("qev", esl), qt)

    def ph_bounds(self, L):
        T, SB, PS = self.T, self.SB, self.PS
        nkh = 1 if L == 0 else NHL
        for (nh, srcT, dst) in ((NHL, T["qT_hbm"], SB["mq2"]),
                                (nkh, T["kT_hbm"], SB["mk2"])):
            for h in range(nh):
                self.w("sync", "sdve")
                self.dma("sync", SB["buf32k"][:, 0:4096],
                         srcT[h * 128:(h + 1) * 128, :])
                self.w("dve", "qld")
                for g in range(8):
                    sl = g % 2
                    self.w("dve", "spe")
                    self.op("dve", lambda e, g=g, sl=sl: e.tensor_mul(
                        SB["sqb"][:, sl, :],
                        SB["buf32k"][:, g * TT:(g + 1) * TT],
                        SB["buf32k"][:, g * TT:(g + 1) * TT]))
                    pb = g % 2
                    self.w("pe", "sdve")
                    self.w("pe", "sact")
                    self.mm(PS[pb][:, :], SB["ones"][:, :], SB["sqb"][:, sl, :],
                            start=True, stop=True, inc=True)
                    self.w("dve", "spe")
                    self.op("dve", lambda e, pb=pb: e.reduce_max(
                        SB["nt"][:, 0:1], PS[pb][:, :], mybir.AxisListType.X))
                    if g == 0:
                        self.op("dve", lambda e, h=h, dst=dst: e.tensor_copy(
                            dst[:, h:h + 1], SB["nt"][:, 0:1]))
                    else:
                        self.op("dve", lambda e, h=h, dst=dst: e.tensor_max(
                            dst[:, h:h + 1], dst[:, h:h + 1], SB["nt"][:, 0:1]))
        self.w("act", "sdve")
        for h in range(NHL):
            kh = 0 if L == 0 else h
            self.op("act", lambda e, h=h: e.activation(
                SB["nt"][:, 1:2], SB["mq2"][:, h:h + 1], AF.Ln))
            self.op("act", lambda e, kh=kh: e.activation(
                SB["nt"][:, 2:3], SB["mk2"][:, kh:kh + 1], AF.Ln))
            self.w("dve", "sact")
            self.op("dve", lambda e: e.tensor_add(
                SB["nt"][:, 3:4], SB["nt"][:, 1:2], SB["nt"][:, 2:3]))
            self.w("act", "sdve")
            self.op("act", lambda e: e.activation(
                SB["nt"][:, 4:5], SB["nt"][:, 3:4], AF.Exp, scale=0.5))
            self.w("dve", "sact")
            self.op("dve", lambda e, h=h: e.tensor_scalar_mul(
                SB["bh"][:, h:h + 1], SB["nt"][:, 4:5], -SMS))

    def ph_attn(self, L):
        T, SB, PS = self.T, self.SB, self.PS
        for h in range(NHL):
            kv = 0 if L == 0 else h
            ksl = h % 2
            koff = ksl * 4096
            voff = 8192 + ksl * 4096
            qs = f"qs{ksl}"
            if not (L == 0 and h > 0):
                pu = self.lu.get(("kvs", ksl))
                if pu:
                    self.wv("sync", *pu)
                self.dma("sync", SB["buf32k"][:, koff:koff + 4096],
                         T["kT_hbm"][kv * 128:(kv + 1) * 128, :], sem=qs)
                for tc in range(32):
                    self.dma("sync", SB["buf32k"][:, voff + tc * 128:
                                                  voff + (tc + 1) * 128],
                             T["v_hbm"][kv, tc], sem=qs)
            if L == 0 and h > 0:
                koff = 0
                voff = 8192
                ksl = 0
                qs = "qs0"
            ldk = self.c[qs]
            for g in range(8):
                qsl = g % 2
                qq = f"qq{qsl}"
                self.w("sync", "spe")
                self.dma("sync", SB["qg"][:, qsl, :],
                         T["qT_hbm"][h * 128:(h + 1) * 128,
                                     g * TT:(g + 1) * TT], sem=qq)
                ldq = self.c[qq]
                nkc = 4 * g + 4
                bs_ = 2
                ba_ = 3 + 2 * (g % 2)
                for kc in range(nkc):
                    sb = (0, 1, 4)[kc % 3]
                    if kc == 0:
                        self.wv("pe", qq, ldq)
                        self.wv("pe", qs, ldk)
                        self.wbank("pe", ("sumb",))
                        self.wbank("pe", ("avb", g % 2))
                    self.wbank("pe", ("scb", sb))
                    self.mm(PS[sb][:, :],
                            SB["buf32k"][:, koff + kc * 128:koff + (kc + 1) * 128],
                            SB["qg"][:, qsl, :], start=True, stop=True,
                            inc=True)
                    psl = kc % 3
                    self.wbank("act", ("pslot", psl))
                    self.w("act", "spe")
                    self.op("act", lambda e, sb=sb, psl=psl, h=h: e.activation(
                        SB["pbuf"][:, psl, :], PS[sb][:, :], AF.Exp,
                        bias=SB["bh"][:, h:h + 1], scale=SMS))
                    self.rbank(("scb", sb), "sact")
                    if kc >= 4 * g:
                        o = kc - 4 * g
                        msl = kc % 2
                        self.wbank("dve", ("pm", msl))
                        self.w("dve", "sact")
                        self.op("dve", lambda e, psl=psl, msl=msl, o=o:
                                e.tensor_mul(SB["pmask"][:, msl, :],
                                             SB["pbuf"][:, psl, :],
                                             SB["mask"][:, o, :]))
                        self.rbank(("pslot", psl), "sdve")
                        used = SB["pmask"][:, msl, :]
                        self.w("pe", "sdve")
                    else:
                        used = SB["pbuf"][:, psl, :]
                        self.w("pe", "sact")
                    self.mm(PS[bs_][:, :], SB["ones"][:, :], used,
                            start=(kc == 0), stop=(kc == nkc - 1))
                    self.mm(PS[ba_][:, :],
                            SB["buf32k"][:, voff + kc * 128:voff + (kc + 1) * 128],
                            used, start=(kc == 0), stop=(kc == nkc - 1),
                            inc=True)
                    if kc >= 4 * g:
                        self.rbank(("pm", kc % 2), "spe")
                    else:
                        self.rbank(("pslot", psl), "spe")
                self.wbank("act", ("rsum",))
                self.w("act", "spe")
                self.op("act", lambda e, bs_=bs_: e.activation(
                    SB["lnt"][:, :], PS[bs_][:, :], AF.Ln))
                self.op("act", lambda e: e.activation(
                    SB["rsum"][:, 0, :], SB["lnt"][:, :], AF.Exp,
                    scale=-1.0))
                self.rbank(("sumb",), "sact")
                self.wbank("dve", ("aev", g % 2))
                self.w("dve", "sact")
                self.op("dve", lambda e, ba_=ba_, g=g: e.tensor_mul(
                    SB["attn_ev"][:, g % 2, :], PS[ba_][:, :],
                    SB["rsum"][:, 0, :]))
                self.rbank(("avb", g % 2), "sdve")
                self.rbank(("rsum",), "sdve")
                self.w("act", "sdve")
                qt = f"qt{g % 2}"
                self.dma("scalar",
                         T["attnT_hbm"][h * 128:(h + 1) * 128,
                                        g * TT:(g + 1) * TT],
                         SB["attn_ev"][:, g % 2, :], sem=qt)
                self.rbank(("aev", g % 2), qt)
            self.lu[("kvs", h % 2)] = ("spe", self.c["spe"])

    def ph_outproj(self, L):
        T, SB, PS = self.T, self.SB, self.PS
        AB = SB["actbuf"]
        wo = T["wo_pk"][L]
        for hc in range(NHL):
            self.dma("sync", AB[:, hc * 4096:(hc + 1) * 4096],
                     T["attnT_hbm"][hc * 128:(hc + 1) * 128, :])
        for mb in range(NFC):
            sl = mb % 2
            pu = self.lu.get(("wblk", sl))
            if pu:
                self.wv("pool", *pu)
            qp = f"qp{sl}"
            self.dma("pool", SB["wblk"][:, sl, 0:8, :], wo[mb], sem=qp)
            ldc = self.c[qp]
            for t8 in range(8):
                pb = t8 % 2
                if t8 == 0:
                    self.wv("pe", qp, ldc)
                    self.w("pe", "qld")
                self.wbank("pe", ("qb", pb))
                for kc in range(NHL):
                    self.mm(PS[pb][:, :], SB["wblk"][:, sl, kc, :],
                            AB[:, kc * 4096 + t8 * TT:
                               kc * 4096 + t8 * TT + TT],
                            start=(kc == 0), stop=(kc == NHL - 1),
                            inc=(kc == NHL - 1))
                esl = t8 % 2
                qt = f"qt{esl}"
                self.wbank("dve", ("qev", esl))
                self.w("dve", "spe")
                self.op("dve", lambda e, pb=pb, esl=esl: e.tensor_copy(
                    SB["qev"][:, esl, :], PS[pb][:, :]))
                self.rbank(("qb", pb), "sdve")
                self.w("act", "sdve")
                self.dma("scalar",
                         T["rs_in"][t8 // 4, mb * 128:(mb + 1) * 128,
                                    (t8 % 4) * TT:(t8 % 4 + 1) * TT],
                         SB["qev"][:, esl, :], sem=qt)
                self.rbank(("qev", esl), qt)
            self.lu[("wblk", sl)] = ("spe", self.c["spe"])
        self.w("pool", "qst", "qt0", "qt1")
        self.op("pool", lambda e: e.collective_compute(
            "ReduceScatter", ALU.add,
            replica_groups=[[0, 1], [2, 3], [4, 5], [6, 7]],
            ins=[T["rs_in"][:, :, :]], outs=[T["rs_out"][:, :]]), inc="cc")

    def ph_resid(self, L):
        T, SB = self.T, self.SB
        dstT = T["x1T"] if L == 0 else T["outT"]
        self.wv("sync", "cc", self.c["cc"])
        for t in range(NT_OWN):
            self.w("sync", "sdve")
            for fc in range(NFC):
                self.dma("sync", SB["xnb"][:, fc, :],
                         T["rs_out"][fc * 128:(fc + 1) * 128,
                                     t * TT:(t + 1) * TT])
                self.dma("sync", SB["res32"][:, fc, :],
                         T["x1pT"][fc * 128:(fc + 1) * 128,
                                   t * TT:(t + 1) * TT])
            self.w("dve", "qld")
            for fc in range(NFC):
                bcol = SB["bias"][:, 9 + fc:10 + fc] if L == 0 \
                    else SB["bias"][:, 25:26]
                osl = fc % 2
                qt = f"qt{osl}"
                self.wbank("dve", ("o32", osl))
                self.op("dve", lambda e, fc=fc, osl=osl, b=bcol:
                        e.scalar_tensor_tensor(
                            SB["o32"][:, osl, :], SB["xnb"][:, fc, :], b,
                            SB["res32"][:, fc, :], ALU.add, ALU.add))
                self.w("act", "sdve")
                self.dma("scalar",
                         dstT[fc * 128:(fc + 1) * 128, t * TT:(t + 1) * TT],
                         SB["o32"][:, osl, :], sem=qt)
                self.rbank(("o32", osl), qt)



def build_program(mode, taps=(), stoplim=None, skipset=None):
    layers = {"FUSED": [0, 1], "L0": [0], "L1": [1]}[mode]
    nc = bass.Bass("TRN2", target_bir_lowering=False, debug=False,
                   dynamic_dma_scratch_size=16384)
    T = {}

    def dram(name, shape, dtype, kind="Internal"):
        if name in taps and kind == "Internal":
            kind = "ExternalOutput"
        T[name] = nc.dram_tensor(name, shape, dtype, kind=kind).ap()
        return T[name]

    if 0 in layers:
        dram("x0T_full", [E, S], F32, "ExternalInput")
        dram("x0T_own", [E, 2048], F32, "ExternalInput")
        dram("su_ug_t", [128, 16], F32, "ExternalInput")
        dram("wsc_ug_t", [128, 128], F32, "ExternalInput")
        dram("su_dp_t", [128, 64], F32, "ExternalInput")
        dram("wsc_dp_t", [128, 16], F32, "ExternalInput")
        dram("bq0_t", [128, 8], F32, "ExternalInput")
        dram("bk0_t", [128, 1], F32, "ExternalInput")
        dram("bfc_t", [128, 16], F32, "ExternalInput")
        dram("bv0_row", [1, 128], F32, "ExternalInput")
    T["wq_pk"], T["wk_pk"], T["wv_rhs"], T["wo_pk"] = {}, {}, {}, {}
    T["wup_pk"], T["wgate_pk"], T["wdn_pk"] = {}, {}, {}
    for L in layers:
        T["wq_pk"][L] = dram(f"wq{L}", [8, 128, 16, 128], F32, "ExternalInput")
        T["wk_pk"][L] = dram(f"wk{L}", [1 if L == 0 else 8, 128, 16, 128],
                             F32, "ExternalInput")
        T["wv_rhs"][L] = dram(f"wv{L}", [16, 128, 128 if L == 0 else 1024],
                              F32, "ExternalInput")
        T["wo_pk"][L] = dram(f"wo{L}", [16, 128, 8, 128], F32, "ExternalInput")
        T["wup_pk"][L] = dram(f"wup{L}", [64, 128, 16, 128], F32,
                              "ExternalInput")
        T["wgate_pk"][L] = dram(f"wgate{L}", [64, 128, 16, 128], F32,
                                "ExternalInput")
        T["wdn_pk"][L] = dram(f"wdn{L}", [16, 128, 64, 128], F32,
                              "ExternalInput")
    if mode == "L0":
        dram("x1T", [E, 2048], F32, "ExternalOutput")
    elif mode == "L1":
        dram("x1T", [E, 2048], F32, "ExternalInput")
    else:
        dram("x1T", [E, 2048], F32, "Internal")
    if 1 in layers:
        dram("outT", [E, 2048], F32, "ExternalOutput")
    dram("xnT_hbm", [E, S], BF16)
    dram("qT_hbm", [1024, S], BF16)
    dram("kT_hbm", [1024, S], BF16)
    dram("v_hbm", [8, 32, 128, 128], BF16)
    dram("attnT_hbm", [1024, S], BF16)
    dram("x1pT", [E, 2048], F32)
    dram("yH", [64, 128, 2048], BF16)
    dram("rs_in", [2, E, 2048], BF16)
    dram("rs_out", [E, 2048], BF16)
    if "dbga" in taps:
        dram("dbga", [E, 2048], BF16)
    if 1 in layers:
        dram("xn1_own_cc", [E, 2048], BF16)
        dram("xn1_full_cc", [2 * E, 2048], BF16)

    C = {}
    C["h128"] = nc.inline_tensor(_had(128).astype(BF16NP), name="h128c").ap()
    C["ones"] = nc.inline_tensor(np.ones((128, 128), np.float32).astype(BF16NP),
                                 name="onesc").ap()
    mask = np.zeros((4, 128, 512), np.float32)
    for o in range(4):
        for p in range(128):
            mask[o, p, o * 128 + p:] = 1.0
    C["mask"] = nc.inline_tensor(mask.astype(BF16NP), name="maskc").ap()

    from contextlib import ExitStack
    _sb_specs = [
        ("actbuf", [128, 32768], BF16), ("buf32k", [128, 16384], BF16),
        ("wblk", [128, 4, 16, 128], BF16), ("res32", [128, 16, 512], F32),
        ("xnb", [128, 16, 512], BF16), ("xcb", [128, 2, 512], BF16),
        ("sqb", [128, 2, 512], BF16), ("xf", [128, 1, 512], F32),
        ("btmp", [128, 4, 512], BF16), ("mu", [128, 512], F32),
        ("var", [128, 512], F32), ("lnt", [128, 512], F32),
        ("rstd", [128, 512], F32), ("silu", [128, 2, 512], BF16),
        ("ybuf", [128, 2, 512], BF16), ("o32", [128, 2, 512], F32),
        ("qev", [128, 2, 512], BF16), ("qg", [128, 2, 512], BF16),
        ("pbuf", [128, 3, 512], BF16), ("pmask", [128, 2, 512], BF16),
        ("rsum", [128, 1, 512], F32), ("attn_ev", [128, 2, 512], BF16),
        ("mask", [128, 4, 512], BF16), ("h128", [128, 128], BF16),
        ("ones", [128, 128], BF16), ("su", [128, 16], F32),
        ("alpha", [128, 160], F32),
        ("bias", [128, 26], F32), ("bvrow", [1, 128], BF16),
        ("mq2", [128, 8], F32), ("mk2", [128, 8], F32),
        ("bh", [128, 8], F32), ("nt", [128, 8], F32),
    ]
    _sem_names = ("qld", "qst", "qgp", "cc", "spe", "sact", "sdve",
                  "qp0", "qp1", "qs0", "qs1", "qq0", "qq1", "qt0", "qt1",
                  "spool", "bpe", "bact", "bdve", "bpool", "bsync")
    with ExitStack() as _es:
        SB = {n: _es.enter_context(nc.sbuf_tensor(n, sh, dt))
              for (n, sh, dt) in _sb_specs}
        PS = [_es.enter_context(nc.psum_tensor(f"p{i}", [128, 512], F32))
              for i in range(6)]
        SEM = {n: _es.enter_context(nc.semaphore(n)) for n in _sem_names}
        block = _es.enter_context(nc.Block())
        prog = Prog(nc, layers, T, C, SB, PS, SEM)
        prog.stoplim = stoplim
        prog.skipset = skipset

        @block.tensor
        def _(eng):
            prog.start("pe", eng)
            prog.program()

        @block.scalar
        def _(eng):
            prog.start("act", eng)
            prog.program()

        @block.vector
        def _(eng):
            prog.start("dve", eng)
            prog.program()

        @block.gpsimd
        def _(eng):
            prog.start("pool", eng)
            prog.program()

        @block.sync
        def _(eng):
            prog.start("sync", eng)
            prog.program()

    return nc



def _stage_inputs(ins):
    per_core = [dict() for _ in range(8)]
    shared = {}

    w_qkv = np.asarray(ins["L0_attn_w_qkv"])
    b_qkv = np.asarray(ins["L0_attn_b_qkv"])
    w_fc = np.asarray(ins["L0_attn_w_fc"])
    b_fc = np.asarray(ins["L0_attn_b_fc"])
    shared["wk0"] = pack_lhsT(w_qkv[:, 2048:2176])
    shared["wv0"] = pack_rhs(w_qkv[:, 2176:2304])
    shared["bk0_t"] = np.ascontiguousarray(b_qkv[2048:2176].reshape(128, 1))
    shared["bv0_row"] = np.ascontiguousarray(b_qkv[2176:2304].reshape(1, 128))
    shared["bfc_t"] = np.ascontiguousarray(b_fc.reshape(16, 128).T)
    shared["su_ug_t"] = np.ascontiguousarray(
        np.asarray(ins["L0_ffn_SU_ug"]).reshape(16, 128).T)
    wsc = np.asarray(ins["L0_ffn_Wscale_ug"])
    shared["wsc_ug_t"] = np.ascontiguousarray(np.concatenate(
        [wsc[:8192].reshape(64, 128).T, wsc[8192:].reshape(64, 128).T], axis=1))
    shared["su_dp_t"] = np.ascontiguousarray(
        np.asarray(ins["L0_ffn_SU_dp"]).reshape(64, 128).T)
    shared["wsc_dp_t"] = np.ascontiguousarray(
        np.asarray(ins["L0_ffn_Wscale_dp"]).reshape(16, 128).T)
    shared["wup0"] = pack_lhsT(np.asarray(ins["L0_ffn_W_up"]).T)
    shared["wgate0"] = pack_lhsT(np.asarray(ins["L0_ffn_W_gate"]).T)
    shared["wdn0"] = pack_lhsT(np.asarray(ins["L0_ffn_W_down"]).T)
    Wq, Wk, Wv, Wo = (np.asarray(ins["L1_attn_Wq"]), np.asarray(ins["L1_attn_Wk"]),
                      np.asarray(ins["L1_attn_Wv"]), np.asarray(ins["L1_attn_Wo"]))
    wgu = np.asarray(ins["L1_ffn_w_gate_up"])
    shared["wgate1"] = pack_lhsT(wgu[:8192].T)
    shared["wup1"] = pack_lhsT(wgu[8192:].T)
    shared["wdn1"] = pack_lhsT(np.asarray(ins["L1_ffn_w_down"]).T)

    x0 = np.asarray(ins["L0_attn_x"])
    for c in range(8):
        b, par = c >> 1, c & 1
        m = per_core[c]
        x0T = np.ascontiguousarray(x0[b].T)
        m["x0T_full"] = x0T
        m["x0T_own"] = np.ascontiguousarray(x0T[:, par * 2048:(par + 1) * 2048])
        hc = slice(par * 1024, (par + 1) * 1024)
        m["wq0"] = pack_lhsT(w_qkv[:, hc])
        m["wk0"] = shared["wk0"]
        m["wv0"] = shared["wv0"]
        m["wo0"] = pack_lhsT(w_fc[hc, :])
        m["bq0_t"] = np.ascontiguousarray(b_qkv[hc].reshape(8, 128).T)
        for k in ("bk0_t", "bv0_row", "bfc_t", "su_ug_t", "wsc_ug_t",
                  "su_dp_t", "wsc_dp_t"):
            m[k] = shared[k]
        m["wup0"] = shared["wup0"]
        m["wgate0"] = shared["wgate0"]
        m["wdn0"] = shared["wdn0"]
        m["wq1"] = pack_lhsT(Wq[hc, :].T)
        m["wk1"] = pack_lhsT(Wk[hc, :].T)
        m["wv1"] = pack_rhs(Wv[hc, :].T)
        m["wo1"] = pack_lhsT(Wo.T[hc, :])
        m["wup1"] = shared["wup1"]
        m["wgate1"] = shared["wgate1"]
        m["wdn1"] = shared["wdn1"]
    return per_core


_L0_KEYS = ("x0T_full", "x0T_own", "su_ug_t", "wsc_ug_t", "su_dp_t",
            "wsc_dp_t", "bq0_t", "bk0_t", "bfc_t", "bv0_row",
            "wq0", "wk0", "wv0", "wo0", "wup0", "wgate0", "wdn0")
_L1_KEYS = ("wq1", "wk1", "wv1", "wo1", "wup1", "wgate1", "wdn1")

_MODE = "FUSED"


def kernel(**inputs):
    per_core = _stage_inputs(inputs)
    out = np.empty((NB, S, E), dtype=np.float32)
    if _MODE == "FUSED":
        nc = build_program("FUSED")
        in_maps = [{k: per_core[c][k] for k in _L0_KEYS + _L1_KEYS}
                   for c in range(8)]
        res = run_bass_kernel_spmd(nc, in_maps, core_ids=list(range(8)))
        for c in range(8):
            b, par = c >> 1, c & 1
            out[b, par * 2048:(par + 1) * 2048, :] = res.results[c]["outT"].T
    else:
        nc0 = build_program("L0")
        in_maps = [{k: per_core[c][k] for k in _L0_KEYS} for c in range(8)]
        res0 = run_bass_kernel_spmd(nc0, in_maps, core_ids=list(range(8)))
        nc1 = build_program("L1")
        in_maps = [dict(x1T=res0.results[c]["x1T"],
                        **{k: per_core[c][k] for k in _L1_KEYS})
                   for c in range(8)]
        res1 = run_bass_kernel_spmd(nc1, in_maps, core_ids=list(range(8)))
        for c in range(8):
            b, par = c >> 1, c & 1
            out[b, par * 2048:(par + 1) * 2048, :] = res1.results[c]["outT"].T
    return out
